# version 01
# baseline (speedup 1.0000x reference)
from contextlib import ExitStack

import numpy as np
import ml_dtypes

import concourse.bass as bass
import concourse.mybir as mybir
from concourse.bass_utils import run_bass_kernel_spmd

F32 = mybir.dt.float32
BF16 = mybir.dt.bfloat16
I32 = mybir.dt.int32
AF = mybir.ActivationFunctionType
ALU = mybir.AluOpType
BF = ml_dtypes.bfloat16

NC = 8
SEQ = 16384
D = 2048
T = SEQ // NC
KC = D // 128
H = 16
DH = 128
BLK = 256
NB = T // BLK
OC_QKV = 48
INTER = 5504
IC = INTER // 128
TS = 512
NTS = T // TS

_DCHUNKS = [6, 6, 6, 5, 5, 5, 5, 5]
_DSTART = [0]
for _n in _DCHUNKS[:-1]:
    _DSTART.append(_DSTART[-1] + _n)

_EG = 2 * IC * 128 * 128
_EU = _EG
_ED = 6 * 16 * 128 * 128
SEGSZ = _EG + _EU + _ED

ENGS = ("sync", "scalar", "vector", "tensor", "gpsimd")


def _down_owner(j):
    for c in range(NC):
        if j < _DSTART[c] + _DCHUNKS[c]:
            return c, j - _DSTART[c]
    raise AssertionError


class Ctr:
    def __init__(self, h):
        self.h = h
        self.v = 0


class Emit:
    def __init__(self, nc, est):
        self.q = {e: [] for e in ENGS}
        self.bufs = {}
        self.nc = nc
        self.est = est
        self.streams = {}

    def sem(self, name):
        if name not in self.streams:
            self.streams[name] = Ctr(self.est.enter_context(self.nc.semaphore(name)))
        return self.streams[name]

    def op(self, eng, fn, inc=None, n=1):
        if inc is not None:
            inc.v += n
        def run(e, fn=fn, inc=inc, n=n):
            instr = fn(e)
            if inc is not None:
                instr.then_inc(inc.h, n)
        self.q[eng].append(run)

    def dma(self, eng, out, in_, stream):
        c = self.sem(stream)
        c.v += 16
        def run(e, out=out, in_=in_, c=c):
            e.dma_start(out=out, in_=in_).then_inc(c.h, 16)
        self.q[eng].append(run)

    def wait(self, eng, ctr, v=None):
        v = ctr.v if v is None else v
        def run(e, h=ctr.h, v=v):
            e.wait_ge(h, v)
        self.q[eng].append(run)

    def wait_s(self, eng, stream):
        self.wait(eng, self.sem(stream))

    def drain(self, eng, inc=None, n=1):
        if inc is not None:
            inc.v += n
        def run(e, inc=inc, n=n):
            instr = e.drain()
            if inc is not None:
                instr.then_inc(inc.h, n)
        self.q[eng].append(run)

    def barrier(self, esem):
        for name, c in self.streams.items():
            if name.startswith("d_"):
                self.wait("sync", c)
            elif name.startswith("g_"):
                self.wait("gpsimd", c)
        for eng in ENGS:
            c = esem[eng]
            c.v += 1
            def run(e, c=c):
                e.sem_inc(c.h, 1)
            self.q[eng].append(run)
        snap = {eng: esem[eng].v for eng in ENGS}
        for eng in ENGS:
            for other in ENGS:
                if other == eng:
                    continue
                self.wait(eng, esem[other], snap[other])

    def buf_wait(self, eng, key):
        for (ctr, v) in self.bufs.get(key, ()):
            self.wait(eng, ctr, v)

    def buf_set(self, key, *specs):
        out = []
        for s in specs:
            if isinstance(s, str):
                c = self.sem(s)
                out.append((c, c.v))
            else:
                out.append((s, s.v))
        self.bufs[key] = out


def build_program(debug=None):
    nc = bass.Bass("TRN2", target_bir_lowering=False, debug=False)

    x_in = nc.dram_tensor("x", [D, T], F32, kind="ExternalInput").ap()
    wqkv_in = nc.dram_tensor("wqkv", [KC * OC_QKV * 128, 128], F32, kind="ExternalInput").ap()
    wgate_in = nc.dram_tensor("wgate", [KC * KC * 128, 128], F32, kind="ExternalInput").ap()
    wout_in = nc.dram_tensor("wout", [KC * KC * 128, 128], F32, kind="ExternalInput").ap()
    normw_in = nc.dram_tensor("normw", [128, KC], F32, kind="ExternalInput").ap()
    rows_in = nc.dram_tensor("rows", [1, 48], F32, kind="ExternalInput").ap()
    gidx_in = nc.dram_tensor("gidx", [2 * 128, INTER], I32, kind="ExternalInput").ap()
    uidx_in = nc.dram_tensor("uidx", [2 * 128, INTER], I32, kind="ExternalInput").ap()
    didx_in = nc.dram_tensor("didx", [6 * 128, D], I32, kind="ExternalInput").ap()
    out_ext = nc.dram_tensor("out", [D, T], F32, kind="ExternalOutput").ap()

    qs = nc.dram_tensor("qs", [H * 128, T], BF16).ap()
    ks = nc.dram_tensor("ks", [H * 128, T], BF16).ap()
    vts = nc.dram_tensor("vts", [H * NB * 2 * 128, 128], BF16).ap()
    ls = nc.dram_tensor("ls", [H * 128, NB * 128], F32).ap()
    gs = nc.dram_tensor("gs", [D, T], BF16).ap()
    as_s = nc.dram_tensor("as_s", [D, T], BF16).ap()
    ys = nc.dram_tensor("ys", [D, T], F32).ap()
    x3s = nc.dram_tensor("x3s", [D, T], BF16).ap()
    wouts = nc.dram_tensor("wouts", [KC * 128, KC * 128], BF16).ap()
    cc_in = nc.dram_tensor("cc_in", [H * 128, 128], F32).ap()
    cc_out = nc.dram_tensor("cc_out", [NC * H * 128, 128], F32).ap()
    we_in = nc.dram_tensor("we_in", [SEGSZ // 2048, 2048], BF16).ap()
    we_out = nc.dram_tensor("we_out", [NC * SEGSZ // 2048, 2048], BF16).ap()

    dbg = None
    if debug is not None:
        dbg = nc.dram_tensor("dbg", list(debug[1]), debug[2], kind="ExternalOutput").ap()

    est = ExitStack()
    E = Emit(nc, est)

    pe = Ctr(est.enter_context(nc.semaphore("pe")))
    sc = Ctr(est.enter_context(nc.semaphore("sc")))
    ve = Ctr(est.enter_context(nc.semaphore("ve")))
    ccs1 = Ctr(est.enter_context(nc.semaphore("ccs1")))
    ccs2 = Ctr(est.enter_context(nc.semaphore("ccs2")))
    esem = {e: Ctr(est.enter_context(nc.semaphore("b_" + e))) for e in ENGS}

    ident = est.enter_context(nc.sbuf_tensor("ident", [128, 128], BF16))
    ones_k = est.enter_context(nc.sbuf_tensor("ones_k", [128, 1], BF16))
    ones_r = est.enter_context(nc.sbuf_tensor("ones_r", [1, 128], F32))
    consts = est.enter_context(nc.sbuf_tensor("consts", [128, 8], F32))
    rowst = est.enter_context(nc.sbuf_tensor("rowst", [1, 48], F32))
    brc = est.enter_context(nc.sbuf_tensor("brc", [128, 96], F32))
    normw_t = est.enter_context(nc.sbuf_tensor("normw_t", [128, KC], F32))
    powD = est.enter_context(nc.sbuf_tensor("powD", [128, NB * H], F32))
    cfac = est.enter_context(nc.sbuf_tensor("cfac", [128, NC * H], F32))
    iota_j = est.enter_context(nc.sbuf_tensor("iota_j", [128, BLK], F32))
    iota_nm = est.enter_context(nc.sbuf_tensor("iota_nm", [128, 512], F32))
    mskc = est.enter_context(nc.sbuf_tensor("mskc", [128, 512], BF16))
    srow = est.enter_context(nc.sbuf_tensor("srow", [1, TS], F32))
    sqrow = est.enter_context(nc.sbuf_tensor("sqrow", [1, TS], F32))
    arow = est.enter_context(nc.sbuf_tensor("arow", [1, T], F32))
    brow = est.enter_context(nc.sbuf_tensor("brow", [1, T], F32))
    rrow = est.enter_context(nc.sbuf_tensor("rrow", [1, T], F32))
    bc1 = est.enter_context(nc.sbuf_tensor("bc1", [128, TS], F32))
    bc2 = est.enter_context(nc.sbuf_tensor("bc2", [128, TS], F32))

    psA = est.enter_context(nc.psum_tensor("psA", [128, TS], F32))
    psB = est.enter_context(nc.psum_tensor("psB", [128, TS], F32))
    psC = est.enter_context(nc.psum_tensor("psC", [128, TS], F32))
    psD = est.enter_context(nc.psum_tensor("psD", [128, TS], F32))
    psT = est.enter_context(nc.psum_tensor("psT", [128, 256], BF16))
    psT2 = est.enter_context(nc.psum_tensor("psT2", [128, 256], BF16))
    psR = est.enter_context(nc.psum_tensor("psR", [1, TS], F32))
    psAT = est.enter_context(nc.psum_tensor("psAT", [128, TS], F32))

    psum4 = [psA, psB, psC, psD]
    ps_free = {}

    def ps_wait(p):
        f = ps_free.get(id(p))
        if f is not None:
            E.wait("tensor", f[0], f[1])

    def ps_set(p, ctr):
        ps_free[id(p)] = (ctr, ctr.v)

    E.dma("sync", rowst[:, :], rows_in[:, :], "d_misc")
    E.dma("sync", normw_t[:, :], normw_in[:, :], "d_misc")

    def _g0(g):
        g.memset(ident[:], 0.0)
        g.drain()
        g.affine_select(out=ident[:], in_=ident[:], compare_op=ALU.not_equal,
                        fill=1.0, base=0, pattern=[[-1, 128]], channel_multiplier=1)
        g.iota(iota_j[:], pattern=[[1, BLK]], base=0, channel_multiplier=0,
               allow_small_or_imprecise_dtypes=True)
        g.iota(iota_nm[:, 0:256], pattern=[[1, 256]], base=0, channel_multiplier=-1,
               allow_small_or_imprecise_dtypes=True)
        g.iota(iota_nm[:, 256:512], pattern=[[1, 256]], base=-128, channel_multiplier=-1,
               allow_small_or_imprecise_dtypes=True)
        return g.drain()
    E.op("gpsimd", _g0, inc=esem["gpsimd"])

    def _v0(v):
        v.memset(ones_k[:], 1.0)
        v.memset(ones_r[:], 1.0)
        v.memset(consts[:, 0:1], 1e-5)
        v.memset(consts[:, 1:2], 1e-6)
        v.memset(rrow[0:1, :], 0.0)
        return v.drain()
    E.op("vector", _v0, inc=ve)
    E.wait("vector", esem["gpsimd"])
    E.wait_s("vector", "d_misc")
    E.op("vector", lambda v: v.tensor_scalar(mskc[:, :], iota_nm[:, :], 0.0, 1.0, ALU.is_ge, ALU.mult))
    E.drain("vector", inc=ve)

    E.wait("tensor", ve, ve.v)
    E.wait_s("tensor", "d_misc")
    E.op("tensor", lambda t: t.matmul(psC[0:128, 0:48], ones_r[:, :], rowst[:, :],
                                      start=True, stop=True), inc=pe)
    E.wait("vector", pe)
    E.op("vector", lambda v: v.tensor_copy(brc[:, 0:48], psC[0:128, 0:48]))
    E.drain("vector", inc=ve)
    ps_set(psC, ve)
    E.op("vector", lambda v: v.tensor_scalar(brc[:, 48:64], brc[:, 0:16], -1.0, None, ALU.mult))
    E.op("vector", lambda v: v.tensor_scalar(brc[:, 64:80], brc[:, 0:16], -255.0, None, ALU.mult))
    E.drain("vector", inc=ve)

    def _vpow(v):
        for b in range(NB):
            v.tensor_scalar(powD[:, b * 16:(b + 1) * 16], brc[:, 0:16],
                            -256.0 * b, None, ALU.mult)
        for cp in range(NC):
            v.tensor_scalar(cfac[:, cp * 16:(cp + 1) * 16], brc[:, 0:16],
                            brc[:, 32 + cp:33 + cp], -2048.0, ALU.mult, ALU.mult)
        return v.drain()
    E.op("vector", _vpow, inc=ve)
    E.wait("scalar", ve, ve.v)
    E.op("scalar", lambda s: s.activation(out=powD[:, :], in_=powD[:, :], func=AF.Exp))
    E.op("scalar", lambda s: s.activation(out=cfac[:, :], in_=cfac[:, :], func=AF.Exp))
    E.drain("scalar", inc=sc)
    E.wait("vector", sc, sc.v)
    def _vmask(v):
        for cp in range(NC):
            v.tensor_scalar(cfac[:, cp * 16:(cp + 1) * 16], cfac[:, cp * 16:(cp + 1) * 16],
                            brc[:, 40 + cp:41 + cp], None, ALU.mult)
        return v.drain()
    E.op("vector", _vmask, inc=ve)

    E.barrier(esem)

    l2 = ExitStack()
    x2 = l2.enter_context(nc.sbuf_tensor("x2", [128, KC * T], BF16))
    kdec = l2.enter_context(nc.sbuf_tensor("kdec", [128, H * BLK], BF16))
    xst = [l2.enter_context(nc.sbuf_tensor(f"xst_{i}", [128, TS], F32)) for i in range(2)]
    xbf = l2.enter_context(nc.sbuf_tensor("xbf", [128, TS], BF16))
    xsq = l2.enter_context(nc.sbuf_tensor("xsq", [128, TS], BF16))
    wf32 = [l2.enter_context(nc.sbuf_tensor(f"wf32_{i}", [128, KC * 128], F32)) for i in range(2)]
    wbf = [l2.enter_context(nc.sbuf_tensor(f"wbf_{i}", [128, KC * 128], BF16)) for i in range(2)]
    qkvt = [l2.enter_context(nc.sbuf_tensor(f"qkvt_{i}", [128, T], BF16)) for i in range(3)]
    kdt = l2.enter_context(nc.sbuf_tensor("kdt", [128, BLK], BF16))
    tst = l2.enter_context(nc.sbuf_tensor("tst", [128, 512], BF16))
    ctile = [l2.enter_context(nc.sbuf_tensor(f"ctile_{i}", [128, 128], F32)) for i in range(2)]
    L_all = l2.enter_context(nc.sbuf_tensor("L_all", [128, (NB + 1) * 128], F32))
    EW = 2816
    ibuf = l2.enter_context(nc.sbuf_tensor("ibuf", [128, EW], I32))
    ibf = l2.enter_context(nc.sbuf_tensor("ibf", [128, EW], BF16))
    eacc = l2.enter_context(nc.sbuf_tensor("eacc", [128, EW], BF16))
    epl = l2.enter_context(nc.sbuf_tensor("epl", [128, EW], BF16))

    def _kdecgen(s):
        for h in range(H):
            s.activation(out=kdec[:, h * BLK:(h + 1) * BLK], in_=iota_j[:, :],
                         func=AF.Exp, bias=brc[:, 64 + h:65 + h], scale=brc[:, h:h + 1])
        return s.drain()
    E.op("scalar", _kdecgen, inc=sc)

    for ts in range(NTS):
        for kc in range(KC):
            bi = kc % 2
            E.buf_wait("sync", ("xst", bi))
            E.dma("sync", xst[bi][:, :], x_in[kc * 128:(kc + 1) * 128, ts * TS:(ts + 1) * TS], f"d_x{bi}")
            E.wait_s("vector", f"d_x{bi}")
            E.buf_wait("vector", "xbf")
            E.op("vector", lambda v, bi=bi: v.tensor_copy(xbf[:, :], xst[bi][:, :]))
            E.drain("vector", inc=ve)
            E.buf_set(("xst", bi), ve)
            E.wait("scalar", ve, ve.v)
            E.buf_wait("scalar", "xsq")
            E.op("scalar", lambda s: s.activation(out=xsq[:, :], in_=xbf[:, :], func=AF.Square))
            E.drain("scalar", inc=sc)
            if kc == 0:
                ps_wait(psR)
                ps_wait(psAT)
            E.wait("tensor", ve, ve.v)
            E.wait("tensor", sc, sc.v)
            E.op("tensor", lambda t, kc=kc: t.matmul(psR[0:1, 0:TS], ones_k[:, :], xbf[:, :],
                                                     start=(kc == 0), stop=(kc == KC - 1)))
            E.op("tensor", lambda t, kc=kc: t.matmul(psAT[0:1, 0:TS], ones_k[:, :], xsq[:, :],
                                                     start=(kc == 0), stop=(kc == KC - 1)), inc=pe)
            E.buf_set("xbf", pe)
            E.buf_set("xsq", pe)
        E.wait("vector", pe)
        E.buf_wait("vector", "statrows")
        def _rows(v, ts=ts):
            sl = slice(ts * TS, (ts + 1) * TS)
            v.tensor_scalar(srow[0:1, :], psR[0:1, 0:TS], 1.0 / D, None, ALU.mult)
            v.tensor_scalar(sqrow[0:1, :], psAT[0:1, 0:TS], 1.0 / D, None, ALU.mult)
            v.drain()
            v.tensor_tensor(arow[0:1, sl], srow[0:1, :], srow[0:1, :], ALU.mult)
            v.drain()
            v.tensor_tensor(arow[0:1, sl], sqrow[0:1, :], arow[0:1, sl], ALU.subtract)
            return v.drain()
        E.op("vector", _rows, inc=ve)
        ps_set(psR, ve)
        ps_set(psAT, ve)
        E.wait("scalar", ve, ve.v)
        E.op("scalar", lambda s, ts=ts: s.activation(out=arow[0:1, ts * TS:(ts + 1) * TS],
                                                     in_=arow[0:1, ts * TS:(ts + 1) * TS],
                                                     func=AF.Sqrt, bias=consts[0:1, 0:1]))
        E.drain("scalar", inc=sc)
        E.wait("vector", sc, sc.v)
        def _rows2(v, ts=ts):
            sl = slice(ts * TS, (ts + 1) * TS)
            v.reciprocal(arow[0:1, sl], arow[0:1, sl])
            v.drain()
            v.tensor_tensor(brow[0:1, sl], srow[0:1, :], arow[0:1, sl], ALU.mult)
            return v.drain()
        E.op("vector", _rows2, inc=ve)
        E.buf_set("statrows", ve)
        ps_wait(psC)
        ps_wait(psD)
        E.wait("tensor", ve, ve.v)
        E.op("tensor", lambda t, ts=ts: t.matmul(psC[0:128, 0:TS], ones_r[:, :],
                                                 arow[0:1, ts * TS:(ts + 1) * TS], start=True, stop=True))
        E.op("tensor", lambda t, ts=ts: t.matmul(psD[0:128, 0:TS], ones_r[:, :],
                                                 brow[0:1, ts * TS:(ts + 1) * TS], start=True, stop=True),
             inc=pe)
        E.wait("vector", pe)
        E.buf_wait("vector", "bc12")
        def _cpbc(v):
            v.tensor_copy(bc1[:, :], psC[0:128, 0:TS])
            v.tensor_copy(bc2[:, :], psD[0:128, 0:TS])
            return v.drain()
        E.op("vector", _cpbc, inc=ve)
        ps_set(psC, ve)
        ps_set(psD, ve)
        for kc in range(KC):
            bi = kc % 2
            E.buf_wait("sync", ("xst", bi))
            E.dma("sync", xst[bi][:, :], x_in[kc * 128:(kc + 1) * 128, ts * TS:(ts + 1) * TS], f"d_x{bi}")
            E.wait_s("vector", f"d_x{bi}")
            def _norm(v, kc=kc, ts=ts, bi=bi):
                v.tensor_tensor(xst[bi][:, :], xst[bi][:, :], bc1[:, :], ALU.mult)
                v.drain()
                v.tensor_tensor(x2[:, kc * T + ts * TS:kc * T + (ts + 1) * TS],
                                xst[bi][:, :], bc2[:, :], ALU.subtract)
                return v.drain()
            E.op("vector", _norm, inc=ve)
            E.buf_set(("xst", bi), ve)
        E.buf_set("bc12", ve)
    x2_ready = ve.v

    if debug is not None and debug[0] == "x2":
        E.wait("gpsimd", ve, ve.v)
        for kc in range(KC):
            E.dma("gpsimd", dbg[kc * 128:(kc + 1) * 128, :], x2[:, kc * T:(kc + 1) * T], "g_dbg")

    def project_oc(oc, w_in, noc, dst, actfunc, wbuf_i):
        E.buf_wait("sync", ("wf32", wbuf_i))
        for kc in range(KC):
            r = (kc * noc + oc) * 128
            E.dma("sync", wf32[wbuf_i][:, kc * 128:(kc + 1) * 128], w_in[r:r + 128, :], f"d_w{wbuf_i}")
        E.wait_s("scalar", f"d_w{wbuf_i}")
        E.buf_wait("scalar", ("wbf", wbuf_i))
        E.op("scalar", lambda s: s.activation(out=wbf[wbuf_i][:, :], in_=wf32[wbuf_i][:, :],
                                              func=AF.Copy))
        E.drain("scalar", inc=sc)
        E.buf_set(("wf32", wbuf_i), sc)
        sc_w = sc.v
        for ts in range(NTS):
            p = psum4[ts]
            ps_wait(p)
            E.wait("tensor", sc, sc_w)
            E.wait("tensor", ve, x2_ready)
            for kc in range(KC):
                E.op("tensor", lambda t, p=p, kc=kc, ts=ts, wbuf_i=wbuf_i: t.matmul(
                    p[:, :], wbf[wbuf_i][:, kc * 128:(kc + 1) * 128],
                    x2[:, kc * T + ts * TS:kc * T + (ts + 1) * TS],
                    start=(kc == 0), stop=(kc == KC - 1)),
                    inc=(pe if kc == KC - 1 else None))
            if ts == NTS - 1:
                E.buf_set(("wbf", wbuf_i), pe)
            E.wait("scalar", pe)
            E.op("scalar", lambda s, p=p, ts=ts: s.activation(
                out=dst[:, ts * TS:(ts + 1) * TS], in_=p[:, :], func=actfunc))
            E.drain("scalar", inc=sc)
            ps_set(p, sc)

    for h in range(H):
        wi = h % 2
        E.buf_wait("scalar", ("qkvt", 1))
        project_oc(16 + h, wqkv_in, OC_QKV, qkvt[1], AF.Silu, wi)
        sc_k = sc.v
        E.buf_wait("scalar", ("qkvt", 2))
        project_oc(32 + h, wqkv_in, OC_QKV, qkvt[2], AF.Silu, (wi + 1) % 2)
        sc_v = sc.v
        E.buf_wait("scalar", ("qkvt", 0))
        project_oc(h, wqkv_in, OC_QKV, qkvt[0], AF.Silu, wi)
        sc_q = sc.v
        E.wait("gpsimd", sc, sc_k)
        E.dma("gpsimd", ks[h * 128:(h + 1) * 128, :], qkvt[1][:, :], "g_k")
        E.wait("gpsimd", sc, sc_q)
        E.dma("gpsimd", qs[h * 128:(h + 1) * 128, :], qkvt[0][:, :], "g_q")
        E.buf_set(("qkvt", 0), "g_q")
        E.buf_wait("vector", "L_all")
        E.op("vector", lambda v: v.memset(L_all[:, 0:128], 0.0))
        E.drain("vector", inc=ve)
        for b in range(NB):
            E.wait("vector", sc, sc_k)
            E.buf_wait("vector", "kdt")
            E.op("vector", lambda v, h=h, b=b: v.tensor_tensor(
                kdt[:, :], qkvt[1][:, b * BLK:(b + 1) * BLK], kdec[:, h * BLK:(h + 1) * BLK], ALU.mult))
            E.drain("vector", inc=ve)
            ps_wait(psT)
            ps_wait(psT2)
            E.buf_wait("tensor", "tst_w")
            E.wait("tensor", ve, ve.v)
            E.wait("tensor", sc, sc_v)
            for j in range(2):
                E.op("tensor", lambda t, j=j: t.transpose(
                    psT[:, j * 128:(j + 1) * 128], kdt[:, j * 128:(j + 1) * 128], ident[:]))
                E.op("tensor", lambda t, j=j, b=b: t.transpose(
                    psT2[:, j * 128:(j + 1) * 128],
                    qkvt[2][:, b * BLK + j * 128:b * BLK + (j + 1) * 128], ident[:]),
                    inc=(pe if j == 1 else None))
            E.buf_set("kdt", pe)
            E.wait("scalar", pe)
            E.buf_wait("scalar", "tst_w")
            def _evt(s):
                s.activation(out=tst[:, 0:256], in_=psT[:, :], func=AF.Copy)
                s.activation(out=tst[:, 256:512], in_=psT2[:, :], func=AF.Copy)
                return s.drain()
            E.op("scalar", _evt, inc=sc)
            ps_set(psT, sc)
            ps_set(psT2, sc)
            E.wait("gpsimd", sc, sc.v)
            for j in range(2):
                E.dma("gpsimd",
                      vts[(h * NB * 2 + b * 2 + j) * 128:(h * NB * 2 + b * 2 + j + 1) * 128, :],
                      tst[:, 256 + j * 128:256 + (j + 1) * 128], "g_vt")
            half = b % 2
            pcol = slice(half * 128, half * 128 + 128)
            ps_wait(psAT)
            E.wait("tensor", sc, sc.v)
            for j in range(2):
                E.op("tensor", lambda t, j=j, pcol=pcol: t.matmul(
                    psAT[:, pcol], tst[:, j * 128:(j + 1) * 128],
                    tst[:, 256 + j * 128:256 + (j + 1) * 128],
                    start=(j == 0), stop=(j == 1)), inc=(pe if j == 1 else None))
            E.buf_set("tst_w", pe, "g_vt")
            E.wait("scalar", pe)
            E.buf_wait("scalar", ("ctile", half))
            E.op("scalar", lambda s, half=half, pcol=pcol: s.activation(
                out=ctile[half][:, :], in_=psAT[:, pcol], func=AF.Copy))
            E.drain("scalar", inc=sc)
            ps_set(psAT, sc)
            E.wait("vector", sc, sc.v)
            E.op("vector", lambda v, b=b, h=h, half=half: v.scalar_tensor_tensor(
                L_all[:, (b + 1) * 128:(b + 2) * 128], L_all[:, b * 128:(b + 1) * 128],
                powD[:, 16 + h:17 + h], ctile[half][:, :], ALU.mult, ALU.add))
            E.drain("vector", inc=ve)
            E.buf_set(("ctile", half), ve)
        E.buf_set(("qkvt", 1), ve, "g_k")
        E.buf_set(("qkvt", 2), pe)
        E.wait("gpsimd", ve, ve.v)
        E.dma("gpsimd", ls[h * 128:(h + 1) * 128, :], L_all[:, 0:NB * 128], "g_l")
        E.dma("gpsimd", cc_in[h * 128:(h + 1) * 128, :], L_all[:, NB * 128:(NB + 1) * 128], "g_l")
        E.buf_set("L_all", "g_l")

    E.wait_s("gpsimd", "g_l")
    E.op("gpsimd", lambda g: g.collective_compute(
        "AllGather", ALU.bypass, replica_groups=[list(range(NC))],
        ins=[cc_in], outs=[cc_out]), inc=ccs1)

    gtile = qkvt[0]
    for oc in range(KC):
        E.buf_wait("scalar", ("qkvt", 0))
        project_oc(oc, wgate_in, KC, gtile, AF.Sigmoid, oc % 2)
        E.wait("vector", sc, sc.v)
        E.op("vector", lambda v, oc=oc: v.tensor_scalar(
            gtile[:, :], gtile[:, :], normw_t[:, oc:oc + 1], None, ALU.mult))
        E.drain("vector", inc=ve)
        E.wait("gpsimd", ve, ve.v)
        E.dma("gpsimd", gs[oc * 128:(oc + 1) * 128, :], gtile[:, :], "g_g")
        E.buf_set(("qkvt", 0), "g_g")

    for oc in range(KC):
        E.buf_wait("sync", ("wf32", 0))
        for kc in range(KC):
            r = (kc * KC + oc) * 128
            E.dma("sync", wf32[0][:, kc * 128:(kc + 1) * 128], wout_in[r:r + 128, :], "d_w0")
        E.wait_s("scalar", "d_w0")
        E.buf_wait("scalar", ("wbf", 0))
        E.op("scalar", lambda s: s.activation(out=wbf[0][:, :], in_=wf32[0][:, :], func=AF.Copy))
        E.drain("scalar", inc=sc)
        E.buf_set(("wf32", 0), sc)
        E.wait("gpsimd", sc, sc.v)
        E.dma("gpsimd", wouts[oc * 128:(oc + 1) * 128, :], wbf[0][:, :], "g_wo")
        E.buf_set(("wbf", 0), "g_wo")

    exp_tiles = []
    for krow in range(2):
        exp_tiles.append(("g", gidx_in, krow, INTER))
    for krow in range(2):
        exp_tiles.append(("u", uidx_in, krow, INTER))
    for loc in range(6):
        exp_tiles.append(("d", didx_in, loc, D))

    for (mat, src_t, srow_i, W) in exp_tiles:
        widths = [2816, 2688] if W == INTER else [W]
        cbase = 0
        for w in widths:
            E.buf_wait("sync", "ibuf")
            E.dma("sync", ibuf[:, 0:w],
                  src_t[srow_i * 128:(srow_i + 1) * 128, cbase:cbase + w], "d_ib")
            E.wait_s("vector", "d_ib")
            E.buf_wait("vector", "eacc")
            def _expand(v, w=w):
                v.tensor_copy(ibf[:, 0:w], ibuf[:, 0:w])
                v.drain()
                v.tensor_scalar(eacc[:, 0:w], ibf[:, 0:w], 0.0, brc[:, 16:17], ALU.is_equal, ALU.mult)
                for l in range(1, 16):
                    v.drain()
                    v.tensor_scalar(epl[:, 0:w], ibf[:, 0:w], float(l), brc[:, 16 + l:17 + l],
                                    ALU.is_equal, ALU.mult)
                    v.drain()
                    v.tensor_tensor(eacc[:, 0:w], eacc[:, 0:w], epl[:, 0:w], ALU.add)
                return v.drain()
            E.op("vector", _expand, inc=ve)
            E.buf_set("ibuf", ve)
            E.wait("gpsimd", ve, ve.v)
            for ci in range(w // 128):
                c = cbase // 128 + ci
                if mat == "g":
                    off = (c * 2 + srow_i) * 16384
                elif mat == "u":
                    off = _EG + (c * 2 + srow_i) * 16384
                else:
                    off = _EG + _EU + (c * 6 + srow_i) * 16384
                r = off // 2048
                E.dma("gpsimd", we_in[r:r + 8, :], eacc[:, ci * 128:(ci + 1) * 128], "g_we")
            E.buf_set("eacc", "g_we")
            cbase += w

    E.wait_s("gpsimd", "g_we")
    E.op("gpsimd", lambda g: g.collective_compute(
        "AllGather", ALU.bypass, replica_groups=[list(range(NC))],
        ins=[we_in], outs=[we_out]), inc=ccs2)

    if debug is not None and debug[0] == "qkv":
        E.wait_s("gpsimd", "g_q")
        E.wait_s("gpsimd", "g_k")
        E.dma("gpsimd", dbg[0:H * 128, 0:T], qs[:, :], "g_dbg")
        E.dma("gpsimd", dbg[H * 128:2 * H * 128, 0:T], ks[:, :], "g_dbg")

    E.barrier(esem)
    l2.close()

    l3 = ExitStack()
    qdec = l3.enter_context(nc.sbuf_tensor("qdec", [128, H * BLK], BF16))
    ddm = l3.enter_context(nc.sbuf_tensor("ddm", [128, H * 512], BF16))
    ddtmp = l3.enter_context(nc.sbuf_tensor("ddtmp", [128, 512], F32))
    S_t = l3.enter_context(nc.sbuf_tensor("S_t", [128, H * 128], F32))
    cstage = [l3.enter_context(nc.sbuf_tensor(f"cstage_{i}", [128, 128], F32)) for i in range(2)]
    qb_t = [l3.enter_context(nc.sbuf_tensor(f"qb_{i}", [128, BLK], BF16)) for i in range(2)]
    kb_t = [l3.enter_context(nc.sbuf_tensor(f"kb_{i}", [128, BLK], BF16)) for i in range(2)]
    vtb_t = [l3.enter_context(nc.sbuf_tensor(f"vtb_{i}", [128, BLK], BF16)) for i in range(2)]
    lb_t = [l3.enter_context(nc.sbuf_tensor(f"lb_{i}", [128, 128], F32)) for i in range(2)]
    kvt_f = l3.enter_context(nc.sbuf_tensor("kvt_f", [128, 128], F32))
    kvt_b = l3.enter_context(nc.sbuf_tensor("kvt_b", [128, 128], BF16))
    qd_t = l3.enter_context(nc.sbuf_tensor("qd_t", [128, BLK], BF16))
    sd_t = l3.enter_context(nc.sbuf_tensor("sd_t", [128, 512], BF16))
    at_t = [l3.enter_context(nc.sbuf_tensor(f"at_{i}", [128, BLK], BF16)) for i in range(2)]
    sq_t = [l3.enter_context(nc.sbuf_tensor(f"sq_{i}", [128, BLK], BF16)) for i in range(2)]

    def _qdecgen(s):
        for h in range(H):
            s.activation(out=qdec[:, h * BLK:(h + 1) * BLK], in_=iota_j[:, :],
                         func=AF.Exp, bias=brc[:, 48 + h:49 + h], scale=brc[:, 48 + h:49 + h])
        return s.drain()
    E.op("scalar", _qdecgen, inc=sc)
    for h in range(H):
        E.buf_wait("scalar", "ddtmp")
        E.op("scalar", lambda s, h=h: s.activation(out=ddtmp[:, :], in_=iota_nm[:, :],
                                                   func=AF.Exp, scale=brc[:, 48 + h:49 + h]))
        E.drain("scalar", inc=sc)
        E.wait("vector", sc, sc.v)
        E.op("vector", lambda v, h=h: v.tensor_tensor(
            ddm[:, h * 512:(h + 1) * 512], ddtmp[:, :], mskc[:, :], ALU.mult))
        E.drain("vector", inc=ve)
        E.buf_set("ddtmp", ve)

    E.op("vector", lambda v: v.memset(S_t[:, :], 0.0))
    E.drain("vector", inc=ve)
    E.wait("sync", ccs1, 1)
    for cp in range(NC):
        for h in range(H):
            bi = h % 2
            E.buf_wait("sync", ("cstage", bi))
            E.dma("sync", cstage[bi][:, :],
                  cc_out[(cp * H + h) * 128:(cp * H + h + 1) * 128, :], f"d_cs{bi}")
            E.wait_s("vector", f"d_cs{bi}")
            def _sacc(v, cp=cp, h=h, bi=bi):
                v.scalar_tensor_tensor(S_t[:, h * 128:(h + 1) * 128], cstage[bi][:, :],
                                       cfac[:, cp * 16 + h:cp * 16 + h + 1],
                                       S_t[:, h * 128:(h + 1) * 128], ALU.mult, ALU.add)
                return v.drain()
            E.op("vector", _sacc, inc=ve)
            E.buf_set(("cstage", bi), ve)

    if debug is not None and debug[0] == "S":
        E.wait("gpsimd", ve, ve.v)
        E.dma("gpsimd", dbg[0:128, :], S_t[:, :], "g_dbg")

    E.barrier(esem)

    for b in range(NB):
        for h in range(H):
            bi = h % 2
            E.buf_wait("sync", ("pb", bi))
            E.dma("sync", qb_t[bi][:, :], qs[h * 128:(h + 1) * 128, b * BLK:(b + 1) * BLK], f"d_pb{bi}")
            E.dma("sync", kb_t[bi][:, :], ks[h * 128:(h + 1) * 128, b * BLK:(b + 1) * BLK], f"d_pb{bi}")
            for j in range(2):
                E.dma("sync", vtb_t[bi][:, j * 128:(j + 1) * 128],
                      vts[(h * NB * 2 + b * 2 + j) * 128:(h * NB * 2 + b * 2 + j + 1) * 128, :], f"d_pb{bi}")
            E.dma("sync", lb_t[bi][:, :], ls[h * 128:(h + 1) * 128, b * 128:(b + 1) * 128], f"d_pb{bi}")
            E.wait_s("vector", f"d_pb{bi}")
            E.buf_wait("vector", "kvt")
            E.buf_wait("vector", "qd")
            def _kvt(v, b=b, h=h, bi=bi):
                v.scalar_tensor_tensor(kvt_f[:, :], S_t[:, h * 128:(h + 1) * 128],
                                       powD[:, b * 16 + h:b * 16 + h + 1], lb_t[bi][:, :],
                                       ALU.mult, ALU.add)
                v.drain()
                v.tensor_copy(kvt_b[:, :], kvt_f[:, :])
                v.drain()
                v.tensor_tensor(qd_t[:, :], qb_t[bi][:, :], qdec[:, h * BLK:(h + 1) * BLK], ALU.mult)
                return v.drain()
            E.op("vector", _kvt, inc=ve)
            ve_kvt = ve.v
            ps_wait(psA)
            E.wait_s("tensor", f"d_pb{bi}")
            for j in range(2):
                E.op("tensor", lambda t, j=j, bi=bi: t.matmul(
                    psA[:, j * 256:(j + 1) * 256], kb_t[bi][:, j * 128:(j + 1) * 128],
                    qb_t[bi][:, :], start=True, stop=True), inc=(pe if j == 1 else None))
            E.wait("vector", pe)
            E.buf_wait("vector", "sd")
            E.op("vector", lambda v, h=h: v.tensor_tensor(
                sd_t[:, :], psA[:, :], ddm[:, h * 512:(h + 1) * 512], ALU.mult))
            E.drain("vector", inc=ve)
            ps_set(psA, ve)
            half = h % 2
            acol = slice(half * 256, half * 256 + 256)
            ps_wait(psAT)
            E.wait("tensor", ve, ve.v)
            E.op("tensor", lambda t, acol=acol: t.matmul(
                psAT[:, acol], kvt_b[:, :], qd_t[:, :], start=True, stop=False))
            for j in range(2):
                E.op("tensor", lambda t, j=j, bi=bi, acol=acol: t.matmul(
                    psAT[:, acol], vtb_t[bi][:, j * 128:(j + 1) * 128],
                    sd_t[:, j * 256:(j + 1) * 256], start=False, stop=(j == 1)),
                    inc=(pe if j == 1 else None))
            E.buf_set("sd", pe)
            E.buf_set(("pb", bi), pe)
            E.buf_set("kvt", pe)
            E.buf_set("qd", pe)
            E.wait("scalar", pe)
            E.buf_wait("scalar", ("at", half))
            def _eva(s, half=half, acol=acol):
                s.activation(out=at_t[half][:, :], in_=psAT[:, acol], func=AF.Copy)
                s.activation(out=sq_t[half][:, :], in_=psAT[:, acol], func=AF.Square)
                return s.drain()
            E.op("scalar", _eva, inc=sc)
            ps_set(psAT, sc)
            if h == 0:
                ps_wait(psR)
            E.wait("tensor", sc, sc.v)
            E.op("tensor", lambda t, h=h, half=half: t.matmul(
                psR[0:1, 0:256], ones_k[:, :], sq_t[half][:, :],
                start=(h == 0), stop=(h == H - 1)), inc=(pe if h == H - 1 else None))
            E.wait("gpsimd", sc, sc.v)
            E.dma("gpsimd", as_s[h * 128:(h + 1) * 128, b * BLK:(b + 1) * BLK], at_t[half][:, :],
                  f"g_as{half}")
            E.buf_set(("at", half), f"g_as{half}", pe)
        E.wait("vector", pe)
        E.op("vector", lambda v, b=b: v.tensor_copy(rrow[0:1, b * 256:(b + 1) * 256], psR[0:1, 0:256]))
        E.drain("vector", inc=ve)
        ps_set(psR, ve)

    E.barrier(esem)

    if debug is not None and debug[0] == "attn":
        for kc in range(KC):
            E.dma("gpsimd", dbg[kc * 128:(kc + 1) * 128, :], as_s[kc * 128:(kc + 1) * 128, :], "g_dbg")
        E.barrier(esem)

    E.op("vector", lambda v: v.tensor_scalar(arow[0:1, :], rrow[0:1, :], 1.0 / D, None, ALU.mult))
    E.drain("vector", inc=ve)
    E.wait("scalar", ve, ve.v)
    E.op("scalar", lambda s: s.activation(out=arow[0:1, :], in_=arow[0:1, :],
                                          func=AF.Sqrt, bias=consts[0:1, 1:2]))
    E.drain("scalar", inc=sc)
    E.wait("vector", sc, sc.v)
    E.op("vector", lambda v: v.reciprocal(arow[0:1, :], arow[0:1, :]))
    E.drain("vector", inc=ve)

    gated = [l3.enter_context(nc.sbuf_tensor(f"gated_{i}", [128, TS], BF16)) for i in range(KC)]
    att_st = [l3.enter_context(nc.sbuf_tensor(f"attst_{i}", [128, TS], BF16)) for i in range(2)]
    g_st = [l3.enter_context(nc.sbuf_tensor(f"gst_{i}", [128, TS], BF16)) for i in range(2)]
    xr_st = [l3.enter_context(nc.sbuf_tensor(f"xrst_{i}", [128, TS], F32)) for i in range(2)]
    y_st = [l3.enter_context(nc.sbuf_tensor(f"yst_{i}", [128, TS], F32)) for i in range(2)]
    ybf_st = [l3.enter_context(nc.sbuf_tensor(f"ybfst_{i}", [128, TS], BF16)) for i in range(2)]
    ysq_st = [l3.enter_context(nc.sbuf_tensor(f"ysqst_{i}", [128, TS], BF16)) for i in range(2)]
    wost = [l3.enter_context(nc.sbuf_tensor(f"wost_{i}", [128, KC * 128], BF16)) for i in range(2)]
    x3w = [l3.enter_context(nc.sbuf_tensor(f"x3w_{i}", [128, TS], BF16)) for i in range(2)]

    for ts in range(NTS):
        ps_wait(psC)
        E.wait("tensor", ve, ve.v)
        E.op("tensor", lambda t, ts=ts: t.matmul(psC[0:128, 0:TS], ones_r[:, :],
                                                 arow[0:1, ts * TS:(ts + 1) * TS],
                                                 start=True, stop=True), inc=pe)
        E.wait("vector", pe)
        E.buf_wait("vector", "bc12")
        E.op("vector", lambda v: v.tensor_copy(bc1[:, :], psC[0:128, 0:TS]))
        E.drain("vector", inc=ve)
        ps_set(psC, ve)
        for fc in range(KC):
            bi = fc % 2
            E.buf_wait("sync", ("gats", bi))
            E.dma("sync", att_st[bi][:, :], as_s[fc * 128:(fc + 1) * 128, ts * TS:(ts + 1) * TS], f"d_ga{bi}")
            E.dma("sync", g_st[bi][:, :], gs[fc * 128:(fc + 1) * 128, ts * TS:(ts + 1) * TS], f"d_ga{bi}")
            E.buf_wait("vector", ("xr", bi))
            E.wait_s("vector", f"d_ga{bi}")
            def _gat(v, fc=fc, bi=bi):
                v.tensor_tensor(xr_st[bi][:, :], att_st[bi][:, :], g_st[bi][:, :], ALU.mult)
                v.drain()
                v.tensor_tensor(gated[fc][:, :], xr_st[bi][:, :], bc1[:, :], ALU.mult)
                return v.drain()
            E.op("vector", _gat, inc=ve)
            E.buf_set(("gats", bi), ve)
            E.buf_set(("xr", bi), ve)
        ve_gated = ve.v
        E.buf_set("bc12", ve)
        for oc in range(KC):
            bi = oc % 2
            E.buf_wait("sync", ("wost", bi))
            E.dma("sync", wost[bi][:, :], wouts[oc * 128:(oc + 1) * 128, :], f"d_wo{bi}")
            E.buf_wait("sync", ("xr", bi))
            E.dma("sync", xr_st[bi][:, :], x_in[oc * 128:(oc + 1) * 128, ts * TS:(ts + 1) * TS], f"d_xr{bi}")
            p = psum4[oc % 2]
            ps_wait(p)
            E.wait("tensor", ve, ve_gated)
            E.wait_s("tensor", f"d_wo{bi}")
            for fc in range(KC):
                E.op("tensor", lambda t, p=p, fc=fc, bi=bi: t.matmul(
                    p[:, :], wost[bi][:, fc * 128:(fc + 1) * 128],
                    gated[fc][:, :], start=(fc == 0), stop=(fc == KC - 1)),
                    inc=(pe if fc == KC - 1 else None))
            E.buf_set(("wost", bi), pe)
            E.wait("vector", pe)
            E.wait_s("vector", f"d_xr{bi}")
            E.buf_wait("vector", ("y", bi))
            E.op("vector", lambda v, p=p, bi=bi: v.tensor_tensor(
                y_st[bi][:, :], p[:, :], xr_st[bi][:, :], ALU.add))
            E.drain("vector", inc=ve)
            ps_set(p, ve)
            E.buf_set(("xr", bi), ve)
            E.wait("gpsimd", ve, ve.v)
            E.dma("gpsimd", ys[oc * 128:(oc + 1) * 128, ts * TS:(ts + 1) * TS], y_st[bi][:, :], f"g_ys{bi}")
            E.wait("scalar", ve, ve.v)
            E.buf_wait("scalar", ("ybf", bi))
            def _yst(s, bi=bi):
                s.activation(out=ybf_st[bi][:, :], in_=y_st[bi][:, :], func=AF.Copy)
                s.drain()
                s.activation(out=ysq_st[bi][:, :], in_=ybf_st[bi][:, :], func=AF.Square)
                return s.drain()
            E.op("scalar", _yst, inc=sc)
            E.buf_set(("y", bi), sc, f"g_ys{bi}")
            if oc == 0:
                ps_wait(psR)
                ps_wait(psAT)
            E.wait("tensor", sc, sc.v)
            E.op("tensor", lambda t, oc=oc, bi=bi: t.matmul(
                psR[0:1, 0:TS], ones_k[:, :], ybf_st[bi][:, :],
                start=(oc == 0), stop=(oc == KC - 1)))
            E.op("tensor", lambda t, oc=oc, bi=bi: t.matmul(
                psAT[0:1, 0:TS], ones_k[:, :], ysq_st[bi][:, :],
                start=(oc == 0), stop=(oc == KC - 1)), inc=pe)
            E.buf_set(("ybf", bi), pe)
        E.wait("vector", pe)
        E.buf_wait("vector", "statrows")
        def _rows(v, ts=ts):
            sl = slice(ts * TS, (ts + 1) * TS)
            v.tensor_scalar(srow[0:1, :], psR[0:1, 0:TS], 1.0 / D, None, ALU.mult)
            v.tensor_scalar(sqrow[0:1, :], psAT[0:1, 0:TS], 1.0 / D, None, ALU.mult)
            v.drain()
            v.tensor_tensor(arow[0:1, sl], srow[0:1, :], srow[0:1, :], ALU.mult)
            v.drain()
            v.tensor_tensor(arow[0:1, sl], sqrow[0:1, :], arow[0:1, sl], ALU.subtract)
            return v.drain()
        E.op("vector", _rows, inc=ve)
        ps_set(psR, ve)
        ps_set(psAT, ve)
        E.wait("scalar", ve, ve.v)
        E.op("scalar", lambda s, ts=ts: s.activation(out=arow[0:1, ts * TS:(ts + 1) * TS],
                                                     in_=arow[0:1, ts * TS:(ts + 1) * TS],
                                                     func=AF.Sqrt, bias=consts[0:1, 0:1]))
        E.drain("scalar", inc=sc)
        E.wait("vector", sc, sc.v)
        def _rows2(v, ts=ts):
            sl = slice(ts * TS, (ts + 1) * TS)
            v.reciprocal(arow[0:1, sl], arow[0:1, sl])
            v.drain()
            v.tensor_tensor(brow[0:1, sl], srow[0:1, :], arow[0:1, sl], ALU.mult)
            return v.drain()
        E.op("vector", _rows2, inc=ve)
        E.buf_set("statrows", ve)

    for ts in range(NTS):
        ps_wait(psC)
        ps_wait(psD)
        E.wait("tensor", ve, ve.v)
        E.op("tensor", lambda t, ts=ts: t.matmul(psC[0:128, 0:TS], ones_r[:, :],
                                                 arow[0:1, ts * TS:(ts + 1) * TS], start=True, stop=True))
        E.op("tensor", lambda t, ts=ts: t.matmul(psD[0:128, 0:TS], ones_r[:, :],
                                                 brow[0:1, ts * TS:(ts + 1) * TS], start=True, stop=True),
             inc=pe)
        E.wait("vector", pe)
        E.buf_wait("vector", "bc12")
        def _cpbc(v):
            v.tensor_copy(bc1[:, :], psC[0:128, 0:TS])
            v.tensor_copy(bc2[:, :], psD[0:128, 0:TS])
            return v.drain()
        E.op("vector", _cpbc, inc=ve)
        ps_set(psC, ve)
        ps_set(psD, ve)
        for kc in range(KC):
            bi = kc % 2
            E.buf_wait("sync", ("y", bi))
            if ts == 0 and kc == 0:
                E.wait_s("sync", "g_ys0")
                E.wait_s("sync", "g_ys1")
            E.dma("sync", y_st[bi][:, :], ys[kc * 128:(kc + 1) * 128, ts * TS:(ts + 1) * TS], f"d_y{bi}")
            E.buf_wait("vector", ("xr", bi))
            E.wait_s("vector", f"d_y{bi}")
            def _n3(v, kc=kc, ts=ts, bi=bi):
                v.tensor_tensor(xr_st[bi][:, :], y_st[bi][:, :], bc1[:, :], ALU.mult)
                v.drain()
                v.tensor_tensor(x3w[bi][:, :], xr_st[bi][:, :], bc2[:, :], ALU.subtract)
                return v.drain()
            E.op("vector", _n3, inc=ve)
            E.buf_set(("y", bi), ve)
            E.buf_set(("xr", bi), ve)
            E.wait("gpsimd", ve, ve.v)
            E.buf_wait("gpsimd", ("x3w", bi))
            E.dma("gpsimd", x3s[kc * 128:(kc + 1) * 128, ts * TS:(ts + 1) * TS], x3w[bi][:, :], f"g_x3w{bi}")
            E.buf_set(("x3w", bi), f"g_x3w{bi}")
        E.buf_set("bc12", ve)

    if debug is not None and debug[0] == "wexp":
        E.wait("sync", ccs2, 1)
        E.barrier(esem)
        rbase = debug[3] if len(debug) > 3 else 0
        for rr in range(16):
            E.dma("gpsimd", dbg[rr * 128:(rr + 1) * 128, :],
                  we_out[rbase + rr * 128:rbase + (rr + 1) * 128, :], "g_dbg")
    if debug is not None and debug[0] == "x3":
        E.wait_s("gpsimd", "g_x3w0")
        E.wait_s("gpsimd", "g_x3w1")
        for kc in range(KC):
            E.dma("gpsimd", dbg[kc * 128:(kc + 1) * 128, :], x3s[kc * 128:(kc + 1) * 128, :], "g_dbg")
    if debug is not None and debug[0] == "y":
        E.wait_s("gpsimd", "g_ys0")
        E.wait_s("gpsimd", "g_ys1")
        for kc in range(KC):
            E.dma("gpsimd", dbg[kc * 128:(kc + 1) * 128, :], ys[kc * 128:(kc + 1) * 128, :], "g_dbg")

    E.barrier(esem)
    l3.close()

    l5 = ExitStack()
    STW = 1024
    NST = T // STW
    h_sb = l5.enter_context(nc.sbuf_tensor("h_sb", [128, IC * STW], BF16))
    x3t = l5.enter_context(nc.sbuf_tensor("x3t", [128, KC * STW], BF16))
    wg_st = [l5.enter_context(nc.sbuf_tensor(f"wg_{i}", [128, KC * 128], BF16)) for i in range(2)]
    wu_st = [l5.enter_context(nc.sbuf_tensor(f"wu_{i}", [128, KC * 128], BF16)) for i in range(2)]
    wd_st = [l5.enter_context(nc.sbuf_tensor("wd_0", [128, IC * 128], BF16))]
    sg_st = l5.enter_context(nc.sbuf_tensor("sg_st", [128, STW], BF16))
    yf_st = [l5.enter_context(nc.sbuf_tensor(f"yf_{i}", [128, STW], F32)) for i in range(2)]
    of_st = [l5.enter_context(nc.sbuf_tensor(f"of_{i}", [128, STW], F32)) for i in range(2)]

    E.wait("sync", ccs2, 1)
    for st in range(NST):
        E.buf_wait("sync", "x3t")
        for kc in range(KC):
            E.dma("sync", x3t[:, kc * STW:(kc + 1) * STW],
                  x3s[kc * 128:(kc + 1) * 128, st * STW:(st + 1) * STW], "d_x3t")
        for ic in range(IC):
            bi = ic % 2
            E.buf_wait("sync", ("wgu", bi))
            for kc in range(KC):
                seg, krow = kc // 2, kc % 2
                goff = seg * SEGSZ + (ic * 2 + krow) * 16384
                uoff = seg * SEGSZ + _EG + (ic * 2 + krow) * 16384
                E.dma("sync", wg_st[bi][:, kc * 128:(kc + 1) * 128],
                      we_out[goff // 2048:goff // 2048 + 8, :], f"d_wg{bi}")
                E.dma("sync", wu_st[bi][:, kc * 128:(kc + 1) * 128],
                      we_out[uoff // 2048:uoff // 2048 + 8, :], f"d_wg{bi}")
            for p in psum4:
                ps_wait(p)
            E.wait_s("tensor", f"d_wg{bi}")
            E.wait_s("tensor", "d_x3t")
            for kc in range(KC):
                for hf in range(2):
                    col = kc * STW + hf * TS
                    E.op("tensor", lambda t, kc=kc, hf=hf, bi=bi, col=col: t.matmul(
                        [psA, psB][hf][:, :], wg_st[bi][:, kc * 128:(kc + 1) * 128],
                        x3t[:, col:col + TS], start=(kc == 0), stop=(kc == KC - 1)))
                    E.op("tensor", lambda t, kc=kc, hf=hf, bi=bi, col=col: t.matmul(
                        [psC, psD][hf][:, :], wu_st[bi][:, kc * 128:(kc + 1) * 128],
                        x3t[:, col:col + TS], start=(kc == 0), stop=(kc == KC - 1)),
                        inc=(pe if (kc == KC - 1 and hf == 1) else None))
            E.buf_set(("wgu", bi), pe)
            if ic == IC - 1:
                E.buf_set("x3t", pe)
            E.wait("scalar", pe)
            E.buf_wait("scalar", "sg")
            def _sil(s):
                s.activation(out=sg_st[:, 0:TS], in_=psA[:, :], func=AF.Silu)
                s.activation(out=sg_st[:, TS:2 * TS], in_=psB[:, :], func=AF.Silu)
                return s.drain()
            E.op("scalar", _sil, inc=sc)
            ps_set(psA, sc)
            ps_set(psB, sc)
            E.wait("vector", sc, sc.v)
            E.buf_wait("vector", "h_sb")
            def _hm(v, ic=ic):
                v.tensor_tensor(h_sb[:, ic * STW:ic * STW + TS], sg_st[:, 0:TS], psC[:, :], ALU.mult)
                v.drain()
                v.tensor_tensor(h_sb[:, ic * STW + TS:(ic + 1) * STW], sg_st[:, TS:2 * TS],
                                psD[:, :], ALU.mult)
                return v.drain()
            E.op("vector", _hm, inc=ve)
            ps_set(psC, ve)
            ps_set(psD, ve)
            E.buf_set("sg", ve)
        h_ready = ve.v
        if debug is not None and debug[0] == "h" and st == 0:
            E.wait("gpsimd", ve, ve.v)
            E.dma("gpsimd", dbg[:, :], h_sb[:, 0:debug[1][1]], "g_dbg")
        for oc in range(KC):
            bi = oc % 2
            E.buf_wait("sync", "wd")
            for j in range(IC):
                seg, loc = _down_owner(j)
                doff = seg * SEGSZ + _EG + _EU + (oc * 6 + loc) * 16384
                E.dma("sync", wd_st[0][:, j * 128:(j + 1) * 128],
                      we_out[doff // 2048:doff // 2048 + 8, :], "d_wd")
            E.buf_wait("sync", ("yf", bi))
            E.dma("sync", yf_st[bi][:, :], ys[oc * 128:(oc + 1) * 128, st * STW:(st + 1) * STW], f"d_yf{bi}")
            ps_wait(psA)
            ps_wait(psB)
            E.wait_s("tensor", "d_wd")
            E.wait("tensor", ve, h_ready)
            for j in range(IC):
                for hf in range(2):
                    E.op("tensor", lambda t, j=j, hf=hf: t.matmul(
                        [psA, psB][hf][:, :], wd_st[0][:, j * 128:(j + 1) * 128],
                        h_sb[:, j * STW + hf * TS:j * STW + (hf + 1) * TS],
                        start=(j == 0), stop=(j == IC - 1)),
                        inc=(pe if (j == IC - 1 and hf == 1) else None))
            E.buf_set("wd", pe)
            if oc == KC - 1:
                E.buf_set("h_sb", pe)
            E.wait("vector", pe)
            E.wait_s("vector", f"d_yf{bi}")
            E.buf_wait("vector", ("of", bi))
            def _dn(v, bi=bi):
                v.tensor_tensor(of_st[bi][:, 0:TS], psA[:, :], yf_st[bi][:, 0:TS], ALU.add)
                v.drain()
                v.tensor_tensor(of_st[bi][:, TS:2 * TS], psB[:, :], yf_st[bi][:, TS:2 * TS], ALU.add)
                return v.drain()
            E.op("vector", _dn, inc=ve)
            ps_set(psA, ve)
            ps_set(psB, ve)
            E.buf_set(("yf", bi), ve)
            E.wait("gpsimd", ve, ve.v)
            E.buf_wait("gpsimd", ("of", bi))
            E.dma("gpsimd", out_ext[oc * 128:(oc + 1) * 128, st * STW:(st + 1) * STW],
                  of_st[bi][:, :], f"g_out{bi}")
            E.buf_set(("of", bi), f"g_out{bi}")

    E.wait_s("gpsimd", "g_out0")
    E.wait_s("gpsimd", "g_out1")
    if debug is not None:
        E.wait_s("gpsimd", "g_dbg")
    E.barrier(esem)

    with nc.Block() as block:
        @block.sync
        def _(e):
            for fn in E.q["sync"]:
                fn(e)

        @block.scalar
        def _(e):
            for fn in E.q["scalar"]:
                fn(e)

        @block.vector
        def _(e):
            for fn in E.q["vector"]:
                fn(e)

        @block.tensor
        def _(e):
            for fn in E.q["tensor"]:
                fn(e)

        @block.gpsimd
        def _(e):
            for fn in E.q["gpsimd"]:
                fn(e)

    l5.close()
    est.close()
    return nc


def _prep_inputs(inputs):
    x = np.asarray(inputs["L0_attn_hidden_states"])[0]
    slope = np.asarray(inputs["L0_attn_slope_rate"])
    w_qkv = np.asarray(inputs["L0_attn_w_qkv"])
    w_gate = np.asarray(inputs["L0_attn_w_gate"])
    w_out = np.asarray(inputs["L0_attn_w_out"])
    norm_w = np.asarray(inputs["L0_attn_norm_w"])
    lut = np.asarray(inputs["L0_ffn_lut"])
    gidx = np.asarray(inputs["L0_ffn_gate_idx"])
    uidx = np.asarray(inputs["L0_ffn_up_idx"])
    didx = np.asarray(inputs["L0_ffn_down_idx"])

    def chunks(wT, noc):
        return np.ascontiguousarray(
            wT.reshape(KC, 128, noc, 128).transpose(0, 2, 1, 3)).reshape(KC * noc * 128, 128)

    wqkv_c = chunks(np.ascontiguousarray(w_qkv.T), OC_QKV)
    wgate_c = chunks(np.ascontiguousarray(w_gate.T), KC)
    wout_c = chunks(np.ascontiguousarray(w_out.T), KC)
    normw_t = np.ascontiguousarray(norm_w.reshape(KC, 128).T)
    gidxT = np.ascontiguousarray(gidx.T)
    uidxT = np.ascontiguousarray(uidx.T)
    didxT = np.ascontiguousarray(didx.T)

    in_maps = []
    for c in range(NC):
        rows = np.zeros((1, 48), np.float32)
        rows[0, 0:16] = slope
        rows[0, 16:32] = lut
        for cp in range(NC):
            rows[0, 32 + cp] = float(c - 1 - cp) if cp < c else 0.0
            rows[0, 40 + cp] = 1.0 if cp < c else 0.0
        dsl = np.zeros((768, D), np.int32)
        r0 = _DSTART[c] * 128
        nreal = _DCHUNKS[c] * 128
        dsl[:nreal] = didxT[r0:r0 + nreal]
        m = {
            "x": np.ascontiguousarray(x[c * T:(c + 1) * T].T),
            "wqkv": wqkv_c,
            "wgate": wgate_c,
            "wout": wout_c,
            "normw": normw_t,
            "rows": rows,
            "gidx": np.ascontiguousarray(gidxT[c * 256:(c + 1) * 256]),
            "uidx": np.ascontiguousarray(uidxT[c * 256:(c + 1) * 256]),
            "didx": dsl,
        }
        in_maps.append(m)
    return in_maps


def _kernel_impl(inputs, debug=None):
    in_maps = _prep_inputs(inputs)
    nc = build_program(debug=debug)
    res = run_bass_kernel_spmd(nc, in_maps, core_ids=list(range(NC)))
    outs = [r["out"] for r in res.results]
    full = np.empty((1, SEQ, D), np.float32)
    for c in range(NC):
        full[0, c * T:(c + 1) * T, :] = outs[c].T
    if debug is not None:
        return full, [r["dbg"] for r in res.results]
    return full


def kernel(**inputs):
    return _kernel_impl(inputs)


# version 03
# speedup vs baseline: 1.1996x; 1.1996x over previous
from contextlib import ExitStack

import numpy as np
import ml_dtypes

import concourse.bass as bass
import concourse.mybir as mybir
from concourse.bass_utils import run_bass_kernel_spmd

F32 = mybir.dt.float32
BF16 = mybir.dt.bfloat16
I32 = mybir.dt.int32
AF = mybir.ActivationFunctionType
ALU = mybir.AluOpType
BF = ml_dtypes.bfloat16

NC = 8
SEQ = 16384
D = 2048
T = SEQ // NC
KC = D // 128
H = 16
DH = 128
BLK = 256
NB = T // BLK
OC_QKV = 48
INTER = 5504
IC = INTER // 128
TS = 512
NTS = T // TS

_DCHUNKS = [6, 6, 6, 5, 5, 5, 5, 5]
_DSTART = [0]
for _n in _DCHUNKS[:-1]:
    _DSTART.append(_DSTART[-1] + _n)

_EG = 2 * IC * 128 * 128
_EU = _EG
_ED = 6 * 16 * 128 * 128
SEGSZ = _EG + _EU + _ED

ENGS = ("sync", "scalar", "vector", "tensor", "gpsimd")


def _down_owner(j):
    for c in range(NC):
        if j < _DSTART[c] + _DCHUNKS[c]:
            return c, j - _DSTART[c]
    raise AssertionError


class Ctr:
    def __init__(self, h):
        self.h = h
        self.v = 0


class Emit:
    def __init__(self, nc, est):
        self.q = {e: [] for e in ENGS}
        self.bufs = {}
        self.nc = nc
        self.est = est
        self.streams = {}
        self.stream_eng = {}

    def sem(self, name):
        if name not in self.streams:
            self.streams[name] = Ctr(self.est.enter_context(self.nc.semaphore(name)))
        return self.streams[name]

    def op(self, eng, fn, inc=None, n=1):
        if inc is not None:
            inc.v += n
        def run(e, fn=fn, inc=inc, n=n):
            instr = fn(e)
            if inc is not None:
                instr.then_inc(inc.h, n)
        self.q[eng].append(run)

    def dma(self, eng, out, in_, stream):
        c = self.sem(stream)
        c.v += 16
        self.stream_eng[stream] = eng
        def run(e, out=out, in_=in_, c=c):
            e.dma_start(out=out, in_=in_).then_inc(c.h, 16)
        self.q[eng].append(run)

    def wait(self, eng, ctr, v=None):
        v = ctr.v if v is None else v
        def run(e, h=ctr.h, v=v):
            e.wait_ge(h, v)
        self.q[eng].append(run)

    def wait_s(self, eng, stream):
        self.wait(eng, self.sem(stream))

    def drain(self, eng, inc=None, n=1):
        if inc is not None:
            inc.v += n
        def run(e, inc=inc, n=n):
            instr = e.drain()
            if inc is not None:
                instr.then_inc(inc.h, n)
        self.q[eng].append(run)

    def barrier(self, esem, exclude=()):
        engs = [e for e in ENGS if e not in exclude]
        for name, c in self.streams.items():
            eng = self.stream_eng.get(name, "sync")
            if eng in exclude:
                eng = "sync"
            self.wait(eng, c)
        for eng in engs:
            c = esem[eng]
            c.v += 1
            def run(e, c=c):
                e.sem_inc(c.h, 1)
            self.q[eng].append(run)
        snap = {eng: esem[eng].v for eng in engs}
        for eng in engs:
            for other in engs:
                if other == eng:
                    continue
                self.wait(eng, esem[other], snap[other])

    def buf_wait(self, eng, key):
        for (ctr, v) in self.bufs.get(key, ()):
            self.wait(eng, ctr, v)

    def buf_set(self, key, *specs):
        out = []
        for s in specs:
            if isinstance(s, str):
                c = self.sem(s)
                out.append((c, c.v))
            else:
                out.append((s, s.v))
        self.bufs[key] = out


def build_program(debug=None):
    nc = bass.Bass("TRN2", target_bir_lowering=False, debug=False)

    x_in = nc.dram_tensor("x", [D, T], F32, kind="ExternalInput").ap()
    wqkv_in = nc.dram_tensor("wqkv", [KC * OC_QKV * 128, 128], F32, kind="ExternalInput").ap()
    wgate_in = nc.dram_tensor("wgate", [KC * KC * 128, 128], F32, kind="ExternalInput").ap()
    wout_in = nc.dram_tensor("wout", [KC * KC * 128, 128], F32, kind="ExternalInput").ap()
    normw_in = nc.dram_tensor("normw", [128, KC], F32, kind="ExternalInput").ap()
    rows_in = nc.dram_tensor("rows", [1, 48], F32, kind="ExternalInput").ap()
    gidx_in = nc.dram_tensor("gidx", [2 * 128, INTER], I32, kind="ExternalInput").ap()
    uidx_in = nc.dram_tensor("uidx", [2 * 128, INTER], I32, kind="ExternalInput").ap()
    didx_in = nc.dram_tensor("didx", [6 * 128, D], I32, kind="ExternalInput").ap()
    out_ext = nc.dram_tensor("out", [D, T], F32, kind="ExternalOutput").ap()

    qs = nc.dram_tensor("qs", [H * 128, T], BF16).ap()
    ks = nc.dram_tensor("ks", [H * 128, T], BF16).ap()
    vts = nc.dram_tensor("vts", [H * NB * 2 * 128, 128], BF16).ap()
    ls = nc.dram_tensor("ls", [H * 128, NB * 128], F32).ap()
    gs = nc.dram_tensor("gs", [D, T], BF16).ap()
    as_s = nc.dram_tensor("as_s", [D, T], BF16).ap()
    ys = nc.dram_tensor("ys", [D, T], F32).ap()
    x3s = nc.dram_tensor("x3s", [D, T], BF16).ap()
    wouts = nc.dram_tensor("wouts", [KC * 128, KC * 128], BF16).ap()
    cc_in = nc.dram_tensor("cc_in", [128, H * 128], F32).ap()
    cc_out = nc.dram_tensor("cc_out", [NC * 128, H * 128], F32).ap()
    we_in = nc.dram_tensor("we_in", [SEGSZ // 2048, 2048], BF16).ap()
    we_out = nc.dram_tensor("we_out", [NC * SEGSZ // 2048, 2048], BF16).ap()

    dbg = None
    if debug is not None:
        dbg = nc.dram_tensor("dbg", list(debug[1]), debug[2], kind="ExternalOutput").ap()

    est = ExitStack()
    E = Emit(nc, est)

    pe = Ctr(est.enter_context(nc.semaphore("pe")))
    sc = Ctr(est.enter_context(nc.semaphore("sc")))
    ve = Ctr(est.enter_context(nc.semaphore("ve")))
    ccs1 = Ctr(est.enter_context(nc.semaphore("ccs1")))
    ccs2 = Ctr(est.enter_context(nc.semaphore("ccs2")))
    esem = {e: Ctr(est.enter_context(nc.semaphore("b_" + e))) for e in ENGS}

    ident = est.enter_context(nc.sbuf_tensor("ident", [128, 128], BF16))
    ones_k = est.enter_context(nc.sbuf_tensor("ones_k", [128, 1], BF16))
    ones_r = est.enter_context(nc.sbuf_tensor("ones_r", [1, 128], F32))
    consts = est.enter_context(nc.sbuf_tensor("consts", [128, 8], F32))
    rowst = est.enter_context(nc.sbuf_tensor("rowst", [1, 48], F32))
    brc = est.enter_context(nc.sbuf_tensor("brc", [128, 96], F32))
    normw_t = est.enter_context(nc.sbuf_tensor("normw_t", [128, KC], F32))
    powD = est.enter_context(nc.sbuf_tensor("powD", [128, NB * H], F32))
    cfac = est.enter_context(nc.sbuf_tensor("cfac", [128, NC * H], F32))
    iota_j = est.enter_context(nc.sbuf_tensor("iota_j", [128, BLK], F32))
    srow = est.enter_context(nc.sbuf_tensor("srow", [1, TS], F32))
    sqrow = est.enter_context(nc.sbuf_tensor("sqrow", [1, TS], F32))
    arow = est.enter_context(nc.sbuf_tensor("arow", [1, T], F32))
    brow = est.enter_context(nc.sbuf_tensor("brow", [1, T], F32))
    rrow = est.enter_context(nc.sbuf_tensor("rrow", [1, T], F32))
    lmid = ExitStack()
    iota_nm = lmid.enter_context(nc.sbuf_tensor("iota_nm", [128, 512], F32))
    mskc = lmid.enter_context(nc.sbuf_tensor("mskc", [128, 512], BF16))
    bc1 = lmid.enter_context(nc.sbuf_tensor("bc1", [128, TS], F32))
    bc2 = lmid.enter_context(nc.sbuf_tensor("bc2", [128, TS], F32))

    psA = est.enter_context(nc.psum_tensor("psA", [128, TS], F32))
    psB = est.enter_context(nc.psum_tensor("psB", [128, TS], F32))
    psC = est.enter_context(nc.psum_tensor("psC", [128, TS], F32))
    psD = est.enter_context(nc.psum_tensor("psD", [128, TS], F32))
    psT = est.enter_context(nc.psum_tensor("psT", [128, 256], BF16))
    psT2 = est.enter_context(nc.psum_tensor("psT2", [128, 256], BF16))
    psR = est.enter_context(nc.psum_tensor("psR", [1, TS], F32))
    psAT = est.enter_context(nc.psum_tensor("psAT", [128, TS], F32))

    psum4 = [psA, psB, psC, psD]
    ps_free = {}

    def ps_wait(p):
        f = ps_free.get(id(p))
        if f is not None:
            E.wait("tensor", f[0], f[1])

    def ps_set(p, ctr):
        ps_free[id(p)] = (ctr, ctr.v)

    E.dma("sync", rowst[:, :], rows_in[:, :], "d_misc")
    E.dma("sync", normw_t[:, :], normw_in[:, :], "d_misc")

    def _g0(g):
        g.memset(ident[:], 0.0)
        g.drain()
        g.affine_select(out=ident[:], in_=ident[:], compare_op=ALU.not_equal,
                        fill=1.0, base=0, pattern=[[-1, 128]], channel_multiplier=1)
        g.iota(iota_j[:], pattern=[[1, BLK]], base=0, channel_multiplier=0,
               allow_small_or_imprecise_dtypes=True)
        g.iota(iota_nm[:, 0:256], pattern=[[1, 256]], base=0, channel_multiplier=-1,
               allow_small_or_imprecise_dtypes=True)
        g.iota(iota_nm[:, 256:512], pattern=[[1, 256]], base=-128, channel_multiplier=-1,
               allow_small_or_imprecise_dtypes=True)
        return g.drain()
    E.op("gpsimd", _g0, inc=esem["gpsimd"])

    def _v0(v):
        v.memset(ones_k[:], 1.0)
        v.memset(ones_r[:], 1.0)
        v.memset(consts[:, 0:1], 1e-5)
        v.memset(consts[:, 1:2], 1e-6)
        v.memset(rrow[0:1, :], 0.0)
        return v.drain()
    E.op("vector", _v0, inc=ve)
    E.wait("vector", esem["gpsimd"])
    E.wait_s("vector", "d_misc")
    E.op("vector", lambda v: v.tensor_scalar(mskc[:, :], iota_nm[:, :], 0.0, 1.0, ALU.is_ge, ALU.mult))
    E.drain("vector", inc=ve)

    E.wait("tensor", ve, ve.v)
    E.wait_s("tensor", "d_misc")
    E.op("tensor", lambda t: t.matmul(psC[0:128, 0:48], ones_r[:, :], rowst[:, :],
                                      start=True, stop=True), inc=pe)
    E.wait("vector", pe)
    E.op("vector", lambda v: v.tensor_copy(brc[:, 0:48], psC[0:128, 0:48]))
    E.drain("vector", inc=ve)
    ps_set(psC, ve)
    E.op("vector", lambda v: v.tensor_scalar(brc[:, 48:64], brc[:, 0:16], -1.0, None, ALU.mult))
    E.op("vector", lambda v: v.tensor_scalar(brc[:, 64:80], brc[:, 0:16], -255.0, None, ALU.mult))
    E.drain("vector", inc=ve)

    def _vpow(v):
        for b in range(NB):
            v.tensor_scalar(powD[:, b * 16:(b + 1) * 16], brc[:, 0:16],
                            -256.0 * b, None, ALU.mult)
        for cp in range(NC):
            v.tensor_scalar(cfac[:, cp * 16:(cp + 1) * 16], brc[:, 0:16],
                            brc[:, 32 + cp:33 + cp], -2048.0, ALU.mult, ALU.mult)
        return v.drain()
    E.op("vector", _vpow, inc=ve)
    E.wait("scalar", ve, ve.v)
    E.op("scalar", lambda s: s.activation(out=powD[:, :], in_=powD[:, :], func=AF.Exp))
    E.op("scalar", lambda s: s.activation(out=cfac[:, :], in_=cfac[:, :], func=AF.Exp))
    E.drain("scalar", inc=sc)
    E.wait("vector", sc, sc.v)
    def _vmask(v):
        for cp in range(NC):
            v.tensor_scalar(cfac[:, cp * 16:(cp + 1) * 16], cfac[:, cp * 16:(cp + 1) * 16],
                            brc[:, 40 + cp:41 + cp], None, ALU.mult)
        return v.drain()
    E.op("vector", _vmask, inc=ve)

    E.barrier(esem)

    l2 = ExitStack()
    x2 = l2.enter_context(nc.sbuf_tensor("x2", [128, KC * T], BF16))
    kdec = l2.enter_context(nc.sbuf_tensor("kdec", [128, H * BLK], BF16))
    xst = [l2.enter_context(nc.sbuf_tensor(f"xst_{i}", [128, TS], F32)) for i in range(2)]
    xbf = l2.enter_context(nc.sbuf_tensor("xbf", [128, TS], BF16))
    xsq = l2.enter_context(nc.sbuf_tensor("xsq", [128, TS], BF16))
    NW = 4
    wf32 = [l2.enter_context(nc.sbuf_tensor(f"wf32_{i}", [128, KC * 128], F32)) for i in range(NW)]
    wbf = [l2.enter_context(nc.sbuf_tensor(f"wbf_{i}", [128, KC * 128], BF16)) for i in range(NW)]
    qkvt = [l2.enter_context(nc.sbuf_tensor(f"qkvt_{i}", [128, T], BF16)) for i in range(3)]
    kdt = l2.enter_context(nc.sbuf_tensor("kdt", [128, BLK], BF16))
    tst = l2.enter_context(nc.sbuf_tensor("tst", [128, 512], BF16))
    ctile = [l2.enter_context(nc.sbuf_tensor(f"ctile_{i}", [128, 128], F32)) for i in range(2)]
    L_all = l2.enter_context(nc.sbuf_tensor("L_all", [128, (NB + 1) * 128], F32))
    EW = 1408
    ibuf = [l2.enter_context(nc.sbuf_tensor(f"ibuf_{i}", [128, EW], I32)) for i in range(2)]
    ibf = l2.enter_context(nc.sbuf_tensor("ibf", [128, EW], BF16))
    eacc = l2.enter_context(nc.sbuf_tensor("eacc", [128, EW], BF16))
    epl = l2.enter_context(nc.sbuf_tensor("epl", [128, EW], BF16))

    def _kdecgen(s):
        for h in range(H):
            s.activation(out=kdec[:, h * BLK:(h + 1) * BLK], in_=iota_j[:, :],
                         func=AF.Exp, bias=brc[:, 64 + h:65 + h], scale=brc[:, h:h + 1])
        return s.drain()
    E.op("scalar", _kdecgen, inc=sc)

    x2_ts = []
    for ts in range(NTS):
        for kc in range(KC):
            bi = kc % 2
            E.buf_wait("gpsimd", ("xst", bi))
            E.dma("gpsimd", xst[bi][:, :], x_in[kc * 128:(kc + 1) * 128, ts * TS:(ts + 1) * TS], f"d_x{bi}")
            E.wait_s("vector", f"d_x{bi}")
            E.buf_wait("vector", "xbf")
            E.op("vector", lambda v, bi=bi: v.tensor_copy(xbf[:, :], xst[bi][:, :]))
            E.drain("vector", inc=ve)
            E.buf_set(("xst", bi), ve)
            E.wait("scalar", ve, ve.v)
            E.buf_wait("scalar", "xsq")
            E.op("scalar", lambda s: s.activation(out=xsq[:, :], in_=xbf[:, :], func=AF.Square))
            E.drain("scalar", inc=sc)
            if kc == 0:
                ps_wait(psR)
                ps_wait(psAT)
            E.wait("tensor", ve, ve.v)
            E.wait("tensor", sc, sc.v)
            E.op("tensor", lambda t, kc=kc: t.matmul(psR[0:1, 0:TS], ones_k[:, :], xbf[:, :],
                                                     start=(kc == 0), stop=(kc == KC - 1)))
            E.op("tensor", lambda t, kc=kc: t.matmul(psAT[0:1, 0:TS], ones_k[:, :], xsq[:, :],
                                                     start=(kc == 0), stop=(kc == KC - 1)), inc=pe)
            E.buf_set("xbf", pe)
            E.buf_set("xsq", pe)
        E.wait("vector", pe)
        E.buf_wait("vector", "statrows")
        def _rows(v, ts=ts):
            sl = slice(ts * TS, (ts + 1) * TS)
            v.tensor_scalar(srow[0:1, :], psR[0:1, 0:TS], 1.0 / D, None, ALU.mult)
            v.tensor_scalar(sqrow[0:1, :], psAT[0:1, 0:TS], 1.0 / D, None, ALU.mult)
            v.drain()
            v.tensor_tensor(arow[0:1, sl], srow[0:1, :], srow[0:1, :], ALU.mult)
            v.drain()
            v.tensor_tensor(arow[0:1, sl], sqrow[0:1, :], arow[0:1, sl], ALU.subtract)
            return v.drain()
        E.op("vector", _rows, inc=ve)
        ps_set(psR, ve)
        ps_set(psAT, ve)
        E.wait("scalar", ve, ve.v)
        E.op("scalar", lambda s, ts=ts: s.activation(out=arow[0:1, ts * TS:(ts + 1) * TS],
                                                     in_=arow[0:1, ts * TS:(ts + 1) * TS],
                                                     func=AF.Sqrt, bias=consts[0:1, 0:1]))
        E.drain("scalar", inc=sc)
        E.wait("vector", sc, sc.v)
        def _rows2(v, ts=ts):
            sl = slice(ts * TS, (ts + 1) * TS)
            v.reciprocal(arow[0:1, sl], arow[0:1, sl])
            v.drain()
            v.tensor_tensor(brow[0:1, sl], srow[0:1, :], arow[0:1, sl], ALU.mult)
            return v.drain()
        E.op("vector", _rows2, inc=ve)
        E.buf_set("statrows", ve)
        ps_wait(psC)
        ps_wait(psD)
        E.wait("tensor", ve, ve.v)
        E.op("tensor", lambda t, ts=ts: t.matmul(psC[0:128, 0:TS], ones_r[:, :],
                                                 arow[0:1, ts * TS:(ts + 1) * TS], start=True, stop=True))
        E.op("tensor", lambda t, ts=ts: t.matmul(psD[0:128, 0:TS], ones_r[:, :],
                                                 brow[0:1, ts * TS:(ts + 1) * TS], start=True, stop=True),
             inc=pe)
        E.wait("vector", pe)
        E.buf_wait("vector", "bc12")
        def _cpbc(v):
            v.tensor_copy(bc1[:, :], psC[0:128, 0:TS])
            v.tensor_copy(bc2[:, :], psD[0:128, 0:TS])
            return v.drain()
        E.op("vector", _cpbc, inc=ve)
        ps_set(psC, ve)
        ps_set(psD, ve)
        for kc in range(KC):
            bi = kc % 2
            E.buf_wait("gpsimd", ("xst", bi))
            E.dma("gpsimd", xst[bi][:, :], x_in[kc * 128:(kc + 1) * 128, ts * TS:(ts + 1) * TS], f"d_x{bi}")
            E.wait_s("vector", f"d_x{bi}")
            def _norm(v, kc=kc, ts=ts, bi=bi):
                v.tensor_tensor(xst[bi][:, :], xst[bi][:, :], bc1[:, :], ALU.mult)
                v.drain()
                v.tensor_tensor(x2[:, kc * T + ts * TS:kc * T + (ts + 1) * TS],
                                xst[bi][:, :], bc2[:, :], ALU.subtract)
                return v.drain()
            E.op("vector", _norm, inc=ve)
            E.buf_set(("xst", bi), ve)
        E.buf_set("bc12", ve)
        x2_ts.append(ve.v)
    x2_ready = ve.v

    if debug is not None and debug[0] == "x2":
        E.wait("gpsimd", ve, ve.v)
        for kc in range(KC):
            E.dma("gpsimd", dbg[kc * 128:(kc + 1) * 128, :], x2[:, kc * T:(kc + 1) * T], "g_dbg")

    wqkv_v = wqkv_in.rearrange("(kc o p) j -> o p kc j", o=OC_QKV, p=128)
    wgate_v = wgate_in.rearrange("(kc o p) j -> o p kc j", o=KC, p=128)
    wout_v = wout_in.rearrange("(kc o p) j -> o p kc j", o=KC, p=128)

    wslot = [0]

    def load_w(w_v, oc):
        wbuf_i = wslot[0] % NW
        wslot[0] += 1
        E.buf_wait("sync", ("wf32", wbuf_i))
        E.dma("sync", wf32[wbuf_i][:, :], w_v[oc:oc + 1], f"d_w{wbuf_i}")
        E.wait_s("scalar", f"d_w{wbuf_i}")
        E.buf_wait("scalar", ("wbf", wbuf_i))
        E.op("scalar", lambda s: s.activation(out=wbf[wbuf_i][:, :], in_=wf32[wbuf_i][:, :],
                                              func=AF.Copy))
        E.drain("scalar", inc=sc)
        E.buf_set(("wf32", wbuf_i), sc)
        return wbuf_i

    def project_oc(oc, w_v, dst, actfunc):
        wbuf_i = load_w(w_v, oc)
        sc_w = sc.v
        for ts in range(NTS):
            p = psum4[ts]
            ps_wait(p)
            E.wait("tensor", sc, sc_w)
            E.wait("tensor", ve, x2_ts[ts])
            for kc in range(KC):
                E.op("tensor", lambda t, p=p, kc=kc, ts=ts, wbuf_i=wbuf_i: t.matmul(
                    p[:, :], wbf[wbuf_i][:, kc * 128:(kc + 1) * 128],
                    x2[:, kc * T + ts * TS:kc * T + (ts + 1) * TS],
                    start=(kc == 0), stop=(kc == KC - 1)),
                    inc=(pe if kc == KC - 1 else None))
            if ts == NTS - 1:
                E.buf_set(("wbf", wbuf_i), pe)
            E.wait("scalar", pe)
            E.op("scalar", lambda s, p=p, ts=ts: s.activation(
                out=dst[:, ts * TS:(ts + 1) * TS], in_=p[:, :], func=actfunc))
            E.drain("scalar", inc=sc)
            ps_set(p, sc)

    exp_tiles = []
    for krow in range(2):
        exp_tiles.append(("g", gidx_in, krow, INTER))
    for krow in range(2):
        exp_tiles.append(("u", uidx_in, krow, INTER))
    for loc in range(6):
        exp_tiles.append(("d", didx_in, loc, D))

    exp_slices = []
    for (mat, src_t, srow_i, W) in exp_tiles:
        cbase = 0
        while cbase < W:
            w = min(EW, W - cbase)
            exp_slices.append((mat, src_t, srow_i, cbase, w))

            cbase += w

    exp_ctr = [0]

    def emit_exp_slice(spec):
        (mat, src_t, srow_i, cbase, w) = spec
        bi = exp_ctr[0] % 2
        exp_ctr[0] += 1
        E.buf_wait("sync", ("ibuf", bi))
        E.dma("sync", ibuf[bi][:, 0:w],
              src_t[srow_i * 128:(srow_i + 1) * 128, cbase:cbase + w], f"d_ib{bi}")
        E.wait_s("vector", f"d_ib{bi}")
        E.buf_wait("vector", "eacc")
        def _expand(v, w=w, bi=bi):
            v.tensor_copy(ibf[:, 0:w], ibuf[bi][:, 0:w])
            v.drain()
            v.tensor_scalar(eacc[:, 0:w], ibf[:, 0:w], 0.0, brc[:, 16:17], ALU.is_equal, ALU.mult)
            for l in range(1, 16):
                v.drain()
                v.tensor_scalar(epl[:, 0:w], ibf[:, 0:w], float(l), brc[:, 16 + l:17 + l],
                                ALU.is_equal, ALU.mult)
                v.drain()
                v.tensor_tensor(eacc[:, 0:w], eacc[:, 0:w], epl[:, 0:w], ALU.add)
            return v.drain()
        E.op("vector", _expand, inc=ve)
        E.buf_set(("ibuf", bi), ve)
        E.wait("gpsimd", ve, ve.v)
        for ci in range(w // 128):
            c = cbase // 128 + ci
            if mat == "g":
                off = (c * 2 + srow_i) * 16384
            elif mat == "u":
                off = _EG + (c * 2 + srow_i) * 16384
            else:
                off = _EG + _EU + (c * 6 + srow_i) * 16384
            r = off // 2048
            E.dma("gpsimd", we_in[r:r + 8, :], eacc[:, ci * 128:(ci + 1) * 128], "g_we")
        E.buf_set("eacc", "g_we")

    exp_iter = iter(exp_slices)

    for h in range(H):
        E.buf_wait("scalar", ("qkvt", 1))
        project_oc(16 + h, wqkv_v, qkvt[1], AF.Silu)
        sc_k = sc.v
        E.buf_wait("scalar", ("qkvt", 2))
        project_oc(32 + h, wqkv_v, qkvt[2], AF.Silu)
        sc_v = sc.v
        E.buf_wait("scalar", ("qkvt", 0))
        project_oc(h, wqkv_v, qkvt[0], AF.Silu)
        sc_q = sc.v
        E.wait("gpsimd", sc, sc_k)
        E.dma("gpsimd", ks[h * 128:(h + 1) * 128, :], qkvt[1][:, :], "g_k")
        E.wait("gpsimd", sc, sc_q)
        E.dma("gpsimd", qs[h * 128:(h + 1) * 128, :], qkvt[0][:, :], "g_q")
        E.buf_set(("qkvt", 0), "g_q")
        E.buf_wait("vector", "L_all")
        E.op("vector", lambda v: v.memset(L_all[:, 0:128], 0.0))
        E.drain("vector", inc=ve)
        for b in range(NB):
            E.wait("vector", sc, sc_k)
            E.buf_wait("vector", "kdt")
            E.op("vector", lambda v, h=h, b=b: v.tensor_tensor(
                kdt[:, :], qkvt[1][:, b * BLK:(b + 1) * BLK], kdec[:, h * BLK:(h + 1) * BLK], ALU.mult))
            E.drain("vector", inc=ve)
            ps_wait(psT)
            ps_wait(psT2)
            E.buf_wait("tensor", "tst_w")
            E.wait("tensor", ve, ve.v)
            E.wait("tensor", sc, sc_v)
            for j in range(2):
                E.op("tensor", lambda t, j=j: t.transpose(
                    psT[:, j * 128:(j + 1) * 128], kdt[:, j * 128:(j + 1) * 128], ident[:]))
                E.op("tensor", lambda t, j=j, b=b: t.transpose(
                    psT2[:, j * 128:(j + 1) * 128],
                    qkvt[2][:, b * BLK + j * 128:b * BLK + (j + 1) * 128], ident[:]),
                    inc=(pe if j == 1 else None))
            E.buf_set("kdt", pe)
            E.wait("scalar", pe)
            E.buf_wait("scalar", "tst_w")
            def _evt(s):
                s.activation(out=tst[:, 0:256], in_=psT[:, :], func=AF.Copy)
                s.activation(out=tst[:, 256:512], in_=psT2[:, :], func=AF.Copy)
                return s.drain()
            E.op("scalar", _evt, inc=sc)
            ps_set(psT, sc)
            ps_set(psT2, sc)
            E.wait("gpsimd", sc, sc.v)
            for j in range(2):
                E.dma("gpsimd",
                      vts[(h * NB * 2 + b * 2 + j) * 128:(h * NB * 2 + b * 2 + j + 1) * 128, :],
                      tst[:, 256 + j * 128:256 + (j + 1) * 128], "g_vt")
            half = b % 2
            pcol = slice(half * 128, half * 128 + 128)
            ps_wait(psAT)
            E.wait("tensor", sc, sc.v)
            for j in range(2):
                E.op("tensor", lambda t, j=j, pcol=pcol: t.matmul(
                    psAT[:, pcol], tst[:, j * 128:(j + 1) * 128],
                    tst[:, 256 + j * 128:256 + (j + 1) * 128],
                    start=(j == 0), stop=(j == 1)), inc=(pe if j == 1 else None))
            E.buf_set("tst_w", pe, "g_vt")
            E.wait("scalar", pe)
            E.buf_wait("scalar", ("ctile", half))
            E.op("scalar", lambda s, half=half, pcol=pcol: s.activation(
                out=ctile[half][:, :], in_=psAT[:, pcol], func=AF.Copy))
            E.drain("scalar", inc=sc)
            ps_set(psAT, sc)
            E.wait("vector", sc, sc.v)
            E.op("vector", lambda v, b=b, h=h, half=half: v.scalar_tensor_tensor(
                L_all[:, (b + 1) * 128:(b + 2) * 128], L_all[:, b * 128:(b + 1) * 128],
                powD[:, 16 + h:17 + h], ctile[half][:, :], ALU.mult, ALU.add))
            E.drain("vector", inc=ve)
            E.buf_set(("ctile", half), ve)
        E.buf_set(("qkvt", 1), ve, "g_k")
        E.buf_set(("qkvt", 2), pe)
        E.wait("gpsimd", ve, ve.v)
        E.dma("gpsimd", ls[h * 128:(h + 1) * 128, :], L_all[:, 0:NB * 128], "g_l")
        E.dma("gpsimd", cc_in[:, h * 128:(h + 1) * 128], L_all[:, NB * 128:(NB + 1) * 128], "g_l")
        E.buf_set("L_all", "g_l")
        for _ in range(2):
            spec = next(exp_iter, None)
            if spec is not None:
                emit_exp_slice(spec)

    for spec in exp_iter:
        emit_exp_slice(spec)

    E.wait_s("gpsimd", "g_l")
    E.op("gpsimd", lambda g: g.collective_compute(
        "AllGather", ALU.bypass, replica_groups=[list(range(NC))],
        ins=[cc_in], outs=[cc_out]), inc=ccs1)
    E.wait_s("gpsimd", "g_we")
    E.op("gpsimd", lambda g: g.collective_compute(
        "AllGather", ALU.bypass, replica_groups=[list(range(NC))],
        ins=[we_in], outs=[we_out]), inc=ccs2)

    gtile = qkvt[0]
    for oc in range(KC):
        E.buf_wait("scalar", ("qkvt", 0))
        project_oc(oc, wgate_v, gtile, AF.Sigmoid)
        E.wait("vector", sc, sc.v)
        E.op("vector", lambda v, oc=oc: v.tensor_scalar(
            gtile[:, :], gtile[:, :], normw_t[:, oc:oc + 1], None, ALU.mult))
        E.drain("vector", inc=ve)
        E.wait("scalar", ve, ve.v)
        E.dma("scalar", gs[oc * 128:(oc + 1) * 128, :], gtile[:, :], "g_g")
        E.buf_set(("qkvt", 0), "g_g")

    for oc in range(KC):
        wbuf_i = load_w(wout_v, oc)
        E.dma("scalar", wouts[oc * 128:(oc + 1) * 128, :], wbf[wbuf_i][:, :], "g_wo")
        E.buf_set(("wbf", wbuf_i), "g_wo")

    if debug is not None and debug[0] == "qkv":
        E.wait_s("gpsimd", "g_q")
        E.wait_s("gpsimd", "g_k")
        E.dma("gpsimd", dbg[0:H * 128, 0:T], qs[:, :], "g_dbg")
        E.dma("gpsimd", dbg[H * 128:2 * H * 128, 0:T], ks[:, :], "g_dbg")

    E.barrier(esem, exclude=("gpsimd",))
    l2.close()

    l3 = ExitStack()
    qdec = l3.enter_context(nc.sbuf_tensor("qdec", [128, H * BLK], BF16))
    ddm = l3.enter_context(nc.sbuf_tensor("ddm", [128, H * 512], BF16))
    ddtmp = l3.enter_context(nc.sbuf_tensor("ddtmp", [128, 512], F32))
    S_t = l3.enter_context(nc.sbuf_tensor("S_t", [128, H * 128], F32))
    cstage = [l3.enter_context(nc.sbuf_tensor(f"cstage_{i}", [128, H * 128], F32)) for i in range(2)]
    qb_t = [l3.enter_context(nc.sbuf_tensor(f"qb_{i}", [128, BLK], BF16)) for i in range(2)]
    kb_t = [l3.enter_context(nc.sbuf_tensor(f"kb_{i}", [128, BLK], BF16)) for i in range(2)]
    vtb_t = [l3.enter_context(nc.sbuf_tensor(f"vtb_{i}", [128, BLK], BF16)) for i in range(2)]
    lb_t = [l3.enter_context(nc.sbuf_tensor(f"lb_{i}", [128, 128], F32)) for i in range(2)]
    kvt_f = l3.enter_context(nc.sbuf_tensor("kvt_f", [128, 128], F32))
    kvt_b = l3.enter_context(nc.sbuf_tensor("kvt_b", [128, 128], BF16))
    qd_t = l3.enter_context(nc.sbuf_tensor("qd_t", [128, BLK], BF16))
    sd_t = l3.enter_context(nc.sbuf_tensor("sd_t", [128, 512], BF16))
    at_t = [l3.enter_context(nc.sbuf_tensor(f"at_{i}", [128, BLK], BF16)) for i in range(2)]
    sq_t = [l3.enter_context(nc.sbuf_tensor(f"sq_{i}", [128, BLK], BF16)) for i in range(2)]

    def _qdecgen(s):
        for h in range(H):
            s.activation(out=qdec[:, h * BLK:(h + 1) * BLK], in_=iota_j[:, :],
                         func=AF.Exp, bias=brc[:, 48 + h:49 + h], scale=brc[:, 48 + h:49 + h])
        return s.drain()
    E.op("scalar", _qdecgen, inc=sc)
    for h in range(H):
        E.buf_wait("scalar", "ddtmp")
        E.op("scalar", lambda s, h=h: s.activation(out=ddtmp[:, :], in_=iota_nm[:, :],
                                                   func=AF.Exp, scale=brc[:, 48 + h:49 + h]))
        E.drain("scalar", inc=sc)
        E.wait("vector", sc, sc.v)
        E.op("vector", lambda v, h=h: v.tensor_tensor(
            ddm[:, h * 512:(h + 1) * 512], ddtmp[:, :], mskc[:, :], ALU.mult))
        E.drain("vector", inc=ve)
        E.buf_set("ddtmp", ve)

    E.op("vector", lambda v: v.memset(S_t[:, :], 0.0))
    E.drain("vector", inc=ve)
    E.wait("sync", ccs1, 1)
    for cp in range(NC):
        bi = cp % 2
        E.buf_wait("sync", ("cstage", bi))
        E.dma("sync", cstage[bi][:, :], cc_out[cp * 128:(cp + 1) * 128, :], f"d_cs{bi}")
        E.wait_s("vector", f"d_cs{bi}")
        def _sacc(v, cp=cp, bi=bi):
            for h in range(H):
                v.scalar_tensor_tensor(S_t[:, h * 128:(h + 1) * 128],
                                       cstage[bi][:, h * 128:(h + 1) * 128],
                                       cfac[:, cp * 16 + h:cp * 16 + h + 1],
                                       S_t[:, h * 128:(h + 1) * 128], ALU.mult, ALU.add)
                v.drain()
            return v.drain()
        E.op("vector", _sacc, inc=ve)
        E.buf_set(("cstage", bi), ve)

    if debug is not None and debug[0] == "S":
        E.wait("gpsimd", ve, ve.v)
        E.dma("gpsimd", dbg[0:128, :], S_t[:, :], "g_dbg")

    for b in range(NB):
        for h in range(H):
            bi = h % 2
            E.buf_wait("sync", ("pb", bi))
            E.dma("sync", qb_t[bi][:, :], qs[h * 128:(h + 1) * 128, b * BLK:(b + 1) * BLK], f"d_pb{bi}")
            E.dma("sync", kb_t[bi][:, :], ks[h * 128:(h + 1) * 128, b * BLK:(b + 1) * BLK], f"d_pb{bi}")
            for j in range(2):
                E.dma("sync", vtb_t[bi][:, j * 128:(j + 1) * 128],
                      vts[(h * NB * 2 + b * 2 + j) * 128:(h * NB * 2 + b * 2 + j + 1) * 128, :], f"d_pb{bi}")
            E.dma("sync", lb_t[bi][:, :], ls[h * 128:(h + 1) * 128, b * 128:(b + 1) * 128], f"d_pb{bi}")
            E.wait_s("vector", f"d_pb{bi}")
            E.buf_wait("vector", "kvt")
            E.buf_wait("vector", "qd")
            def _kvt(v, b=b, h=h, bi=bi):
                v.scalar_tensor_tensor(kvt_f[:, :], S_t[:, h * 128:(h + 1) * 128],
                                       powD[:, b * 16 + h:b * 16 + h + 1], lb_t[bi][:, :],
                                       ALU.mult, ALU.add)
                v.drain()
                v.tensor_copy(kvt_b[:, :], kvt_f[:, :])
                v.drain()
                v.tensor_tensor(qd_t[:, :], qb_t[bi][:, :], qdec[:, h * BLK:(h + 1) * BLK], ALU.mult)
                return v.drain()
            E.op("vector", _kvt, inc=ve)
            ve_kvt = ve.v
            ps_wait(psA)
            E.wait_s("tensor", f"d_pb{bi}")
            for j in range(2):
                E.op("tensor", lambda t, j=j, bi=bi: t.matmul(
                    psA[:, j * 256:(j + 1) * 256], kb_t[bi][:, j * 128:(j + 1) * 128],
                    qb_t[bi][:, :], start=True, stop=True), inc=(pe if j == 1 else None))
            E.wait("vector", pe)
            E.buf_wait("vector", "sd")
            E.op("vector", lambda v, h=h: v.tensor_tensor(
                sd_t[:, :], psA[:, :], ddm[:, h * 512:(h + 1) * 512], ALU.mult))
            E.drain("vector", inc=ve)
            ps_set(psA, ve)
            half = h % 2
            acol = slice(half * 256, half * 256 + 256)
            ps_wait(psAT)
            E.wait("tensor", ve, ve.v)
            E.op("tensor", lambda t, acol=acol: t.matmul(
                psAT[:, acol], kvt_b[:, :], qd_t[:, :], start=True, stop=False))
            for j in range(2):
                E.op("tensor", lambda t, j=j, bi=bi, acol=acol: t.matmul(
                    psAT[:, acol], vtb_t[bi][:, j * 128:(j + 1) * 128],
                    sd_t[:, j * 256:(j + 1) * 256], start=False, stop=(j == 1)),
                    inc=(pe if j == 1 else None))
            E.buf_set("sd", pe)
            E.buf_set(("pb", bi), pe)
            E.buf_set("kvt", pe)
            E.buf_set("qd", pe)
            E.wait("scalar", pe)
            E.buf_wait("scalar", ("at", half))
            def _eva(s, half=half, acol=acol):
                s.activation(out=at_t[half][:, :], in_=psAT[:, acol], func=AF.Copy)
                s.activation(out=sq_t[half][:, :], in_=psAT[:, acol], func=AF.Square)
                return s.drain()
            E.op("scalar", _eva, inc=sc)
            ps_set(psAT, sc)
            if h == 0:
                ps_wait(psR)
            E.wait("tensor", sc, sc.v)
            E.op("tensor", lambda t, h=h, half=half: t.matmul(
                psR[0:1, 0:256], ones_k[:, :], sq_t[half][:, :],
                start=(h == 0), stop=(h == H - 1)), inc=(pe if h == H - 1 else None))
            E.dma("scalar", as_s[h * 128:(h + 1) * 128, b * BLK:(b + 1) * BLK], at_t[half][:, :],
                  f"g_as{half}")
            E.buf_set(("at", half), f"g_as{half}", pe)
        E.wait("vector", pe)
        E.op("vector", lambda v, b=b: v.tensor_copy(rrow[0:1, b * 256:(b + 1) * 256], psR[0:1, 0:256]))
        E.drain("vector", inc=ve)
        ps_set(psR, ve)

    if debug is not None and debug[0] == "attn":
        for kc in range(KC):
            E.dma("gpsimd", dbg[kc * 128:(kc + 1) * 128, :], as_s[kc * 128:(kc + 1) * 128, :], "g_dbg")
        E.barrier(esem)

    E.wait_s("sync", "g_as0")
    E.wait_s("sync", "g_as1")
    E.wait_s("sync", "g_g")
    E.wait_s("sync", "g_wo")
    E.op("vector", lambda v: v.tensor_scalar(arow[0:1, :], rrow[0:1, :], 1.0 / D, None, ALU.mult))
    E.drain("vector", inc=ve)
    E.wait("scalar", ve, ve.v)
    E.op("scalar", lambda s: s.activation(out=arow[0:1, :], in_=arow[0:1, :],
                                          func=AF.Sqrt, bias=consts[0:1, 1:2]))
    E.drain("scalar", inc=sc)
    E.wait("vector", sc, sc.v)
    E.op("vector", lambda v: v.reciprocal(arow[0:1, :], arow[0:1, :]))
    E.drain("vector", inc=ve)

    gated = [l3.enter_context(nc.sbuf_tensor(f"gated_{i}", [128, TS], BF16)) for i in range(KC)]
    att_st = [l3.enter_context(nc.sbuf_tensor(f"attst_{i}", [128, TS], BF16)) for i in range(2)]
    g_st = [l3.enter_context(nc.sbuf_tensor(f"gst_{i}", [128, TS], BF16)) for i in range(2)]
    xr_st = [l3.enter_context(nc.sbuf_tensor(f"xrst_{i}", [128, TS], F32)) for i in range(2)]
    y_st = [l3.enter_context(nc.sbuf_tensor(f"yst_{i}", [128, TS], F32)) for i in range(2)]
    ybf_st = [l3.enter_context(nc.sbuf_tensor(f"ybfst_{i}", [128, TS], BF16)) for i in range(2)]
    ysq_st = [l3.enter_context(nc.sbuf_tensor(f"ysqst_{i}", [128, TS], BF16)) for i in range(2)]
    wost = [l3.enter_context(nc.sbuf_tensor(f"wost_{i}", [128, KC * 128], BF16)) for i in range(2)]
    x3w = [l3.enter_context(nc.sbuf_tensor(f"x3w_{i}", [128, TS], BF16)) for i in range(2)]

    for ts in range(NTS):
        ps_wait(psC)
        E.wait("tensor", ve, ve.v)
        E.op("tensor", lambda t, ts=ts: t.matmul(psC[0:128, 0:TS], ones_r[:, :],
                                                 arow[0:1, ts * TS:(ts + 1) * TS],
                                                 start=True, stop=True), inc=pe)
        E.wait("vector", pe)
        E.buf_wait("vector", "bc12")
        E.op("vector", lambda v: v.tensor_copy(bc1[:, :], psC[0:128, 0:TS]))
        E.drain("vector", inc=ve)
        ps_set(psC, ve)
        for fc in range(KC):
            bi = fc % 2
            E.buf_wait("sync", ("gats", bi))
            E.dma("sync", att_st[bi][:, :], as_s[fc * 128:(fc + 1) * 128, ts * TS:(ts + 1) * TS], f"d_ga{bi}")
            E.dma("sync", g_st[bi][:, :], gs[fc * 128:(fc + 1) * 128, ts * TS:(ts + 1) * TS], f"d_ga{bi}")
            E.buf_wait("vector", ("xr", bi))
            E.wait_s("vector", f"d_ga{bi}")
            def _gat(v, fc=fc, bi=bi):
                v.tensor_tensor(xr_st[bi][:, :], att_st[bi][:, :], g_st[bi][:, :], ALU.mult)
                v.drain()
                v.tensor_tensor(gated[fc][:, :], xr_st[bi][:, :], bc1[:, :], ALU.mult)
                return v.drain()
            E.op("vector", _gat, inc=ve)
            E.buf_set(("gats", bi), ve)
            E.buf_set(("xr", bi), ve)
        ve_gated = ve.v
        E.buf_set("bc12", ve)
        for oc in range(KC):
            bi = oc % 2
            E.buf_wait("sync", ("wost", bi))
            E.dma("sync", wost[bi][:, :], wouts[oc * 128:(oc + 1) * 128, :], f"d_wo{bi}")
            E.buf_wait("sync", ("xr", bi))
            E.dma("sync", xr_st[bi][:, :], x_in[oc * 128:(oc + 1) * 128, ts * TS:(ts + 1) * TS], f"d_xr{bi}")
            p = psum4[oc % 2]
            ps_wait(p)
            E.wait("tensor", ve, ve_gated)
            E.wait_s("tensor", f"d_wo{bi}")
            for fc in range(KC):
                E.op("tensor", lambda t, p=p, fc=fc, bi=bi: t.matmul(
                    p[:, :], wost[bi][:, fc * 128:(fc + 1) * 128],
                    gated[fc][:, :], start=(fc == 0), stop=(fc == KC - 1)),
                    inc=(pe if fc == KC - 1 else None))
            E.buf_set(("wost", bi), pe)
            E.wait("vector", pe)
            E.wait_s("vector", f"d_xr{bi}")
            E.buf_wait("vector", ("y", bi))
            E.op("vector", lambda v, p=p, bi=bi: v.tensor_tensor(
                y_st[bi][:, :], p[:, :], xr_st[bi][:, :], ALU.add))
            E.drain("vector", inc=ve)
            ps_set(p, ve)
            E.buf_set(("xr", bi), ve)
            E.wait("scalar", ve, ve.v)
            E.dma("scalar", ys[oc * 128:(oc + 1) * 128, ts * TS:(ts + 1) * TS], y_st[bi][:, :], f"g_ys{bi}")
            E.buf_wait("scalar", ("ybf", bi))
            def _yst(s, bi=bi):
                s.activation(out=ybf_st[bi][:, :], in_=y_st[bi][:, :], func=AF.Copy)
                s.drain()
                s.activation(out=ysq_st[bi][:, :], in_=ybf_st[bi][:, :], func=AF.Square)
                return s.drain()
            E.op("scalar", _yst, inc=sc)
            E.buf_set(("y", bi), sc, f"g_ys{bi}")
            if oc == 0:
                ps_wait(psR)
                ps_wait(psAT)
            E.wait("tensor", sc, sc.v)
            E.op("tensor", lambda t, oc=oc, bi=bi: t.matmul(
                psR[0:1, 0:TS], ones_k[:, :], ybf_st[bi][:, :],
                start=(oc == 0), stop=(oc == KC - 1)))
            E.op("tensor", lambda t, oc=oc, bi=bi: t.matmul(
                psAT[0:1, 0:TS], ones_k[:, :], ysq_st[bi][:, :],
                start=(oc == 0), stop=(oc == KC - 1)), inc=pe)
            E.buf_set(("ybf", bi), pe)
        E.wait("vector", pe)
        E.buf_wait("vector", "statrows")
        def _rows(v, ts=ts):
            sl = slice(ts * TS, (ts + 1) * TS)
            v.tensor_scalar(srow[0:1, :], psR[0:1, 0:TS], 1.0 / D, None, ALU.mult)
            v.tensor_scalar(sqrow[0:1, :], psAT[0:1, 0:TS], 1.0 / D, None, ALU.mult)
            v.drain()
            v.tensor_tensor(arow[0:1, sl], srow[0:1, :], srow[0:1, :], ALU.mult)
            v.drain()
            v.tensor_tensor(arow[0:1, sl], sqrow[0:1, :], arow[0:1, sl], ALU.subtract)
            return v.drain()
        E.op("vector", _rows, inc=ve)
        ps_set(psR, ve)
        ps_set(psAT, ve)
        E.wait("scalar", ve, ve.v)
        E.op("scalar", lambda s, ts=ts: s.activation(out=arow[0:1, ts * TS:(ts + 1) * TS],
                                                     in_=arow[0:1, ts * TS:(ts + 1) * TS],
                                                     func=AF.Sqrt, bias=consts[0:1, 0:1]))
        E.drain("scalar", inc=sc)
        E.wait("vector", sc, sc.v)
        def _rows2(v, ts=ts):
            sl = slice(ts * TS, (ts + 1) * TS)
            v.reciprocal(arow[0:1, sl], arow[0:1, sl])
            v.drain()
            v.tensor_tensor(brow[0:1, sl], srow[0:1, :], arow[0:1, sl], ALU.mult)
            return v.drain()
        E.op("vector", _rows2, inc=ve)
        E.buf_set("statrows", ve)

    for ts in range(NTS):
        ps_wait(psC)
        ps_wait(psD)
        E.wait("tensor", ve, ve.v)
        E.op("tensor", lambda t, ts=ts: t.matmul(psC[0:128, 0:TS], ones_r[:, :],
                                                 arow[0:1, ts * TS:(ts + 1) * TS], start=True, stop=True))
        E.op("tensor", lambda t, ts=ts: t.matmul(psD[0:128, 0:TS], ones_r[:, :],
                                                 brow[0:1, ts * TS:(ts + 1) * TS], start=True, stop=True),
             inc=pe)
        E.wait("vector", pe)
        E.buf_wait("vector", "bc12")
        def _cpbc(v):
            v.tensor_copy(bc1[:, :], psC[0:128, 0:TS])
            v.tensor_copy(bc2[:, :], psD[0:128, 0:TS])
            return v.drain()
        E.op("vector", _cpbc, inc=ve)
        ps_set(psC, ve)
        ps_set(psD, ve)
        for kc in range(KC):
            bi = kc % 2
            E.buf_wait("sync", ("y", bi))
            if ts == 0 and kc == 0:
                E.wait_s("sync", "g_ys0")
                E.wait_s("sync", "g_ys1")
            E.dma("sync", y_st[bi][:, :], ys[kc * 128:(kc + 1) * 128, ts * TS:(ts + 1) * TS], f"d_y{bi}")
            E.buf_wait("vector", ("xr", bi))
            E.buf_wait("vector", ("x3w", bi))
            E.wait_s("vector", f"d_y{bi}")
            def _n3(v, kc=kc, ts=ts, bi=bi):
                v.tensor_tensor(xr_st[bi][:, :], y_st[bi][:, :], bc1[:, :], ALU.mult)
                v.drain()
                v.tensor_tensor(x3w[bi][:, :], xr_st[bi][:, :], bc2[:, :], ALU.subtract)
                return v.drain()
            E.op("vector", _n3, inc=ve)
            E.buf_set(("y", bi), ve)
            E.buf_set(("xr", bi), ve)
            E.wait("scalar", ve, ve.v)
            E.dma("scalar", x3s[kc * 128:(kc + 1) * 128, ts * TS:(ts + 1) * TS], x3w[bi][:, :], f"g_x3w{bi}")
            E.buf_set(("x3w", bi), f"g_x3w{bi}")
        E.buf_set("bc12", ve)

    if debug is not None and debug[0] == "wexp":
        E.wait("sync", ccs2, 1)
        E.barrier(esem)
        rbase = debug[3] if len(debug) > 3 else 0
        for rr in range(16):
            E.dma("gpsimd", dbg[rr * 128:(rr + 1) * 128, :],
                  we_out[rbase + rr * 128:rbase + (rr + 1) * 128, :], "g_dbg")
    if debug is not None and debug[0] == "x3":
        E.wait_s("gpsimd", "g_x3w0")
        E.wait_s("gpsimd", "g_x3w1")
        for kc in range(KC):
            E.dma("gpsimd", dbg[kc * 128:(kc + 1) * 128, :], x3s[kc * 128:(kc + 1) * 128, :], "g_dbg")
    if debug is not None and debug[0] == "y":
        E.wait_s("gpsimd", "g_ys0")
        E.wait_s("gpsimd", "g_ys1")
        for kc in range(KC):
            E.dma("gpsimd", dbg[kc * 128:(kc + 1) * 128, :], ys[kc * 128:(kc + 1) * 128, :], "g_dbg")

    E.barrier(esem, exclude=("gpsimd",))
    l3.close()
    lmid.close()

    l5 = ExitStack()
    STW = 1024
    NST = T // STW
    h_sb = l5.enter_context(nc.sbuf_tensor("h_sb", [128, IC * STW], BF16))
    x3t = l5.enter_context(nc.sbuf_tensor("x3t", [128, KC * STW], BF16))
    wg_st = [l5.enter_context(nc.sbuf_tensor(f"wg_{i}", [128, KC * 128], BF16)) for i in range(2)]
    wu_st = [l5.enter_context(nc.sbuf_tensor(f"wu_{i}", [128, KC * 128], BF16)) for i in range(2)]
    wd_st = [l5.enter_context(nc.sbuf_tensor(f"wd_{i}", [128, IC * 128], BF16)) for i in range(2)]
    sg_st = l5.enter_context(nc.sbuf_tensor("sg_st", [128, STW], BF16))
    yf_st = [l5.enter_context(nc.sbuf_tensor(f"yf_{i}", [128, STW], F32)) for i in range(2)]
    of_st = [l5.enter_context(nc.sbuf_tensor(f"of_{i}", [128, STW], F32)) for i in range(2)]

    E.wait("sync", ccs2, 1)
    for st in range(NST):
        E.buf_wait("sync", "x3t")
        for kc in range(KC):
            E.dma("sync", x3t[:, kc * STW:(kc + 1) * STW],
                  x3s[kc * 128:(kc + 1) * 128, st * STW:(st + 1) * STW], "d_x3t")
        for ic in range(IC):
            bi = ic % 2
            E.buf_wait("sync", ("wgu", bi))
            for kc in range(KC):
                seg, krow = kc // 2, kc % 2
                goff = seg * SEGSZ + (ic * 2 + krow) * 16384
                uoff = seg * SEGSZ + _EG + (ic * 2 + krow) * 16384
                E.dma("sync", wg_st[bi][:, kc * 128:(kc + 1) * 128],
                      we_out[goff // 2048:goff // 2048 + 8, :], f"d_wg{bi}")
                E.dma("sync", wu_st[bi][:, kc * 128:(kc + 1) * 128],
                      we_out[uoff // 2048:uoff // 2048 + 8, :], f"d_wg{bi}")
            for p in psum4:
                ps_wait(p)
            E.wait_s("tensor", f"d_wg{bi}")
            E.wait_s("tensor", "d_x3t")
            for kc in range(KC):
                for hf in range(2):
                    col = kc * STW + hf * TS
                    E.op("tensor", lambda t, kc=kc, hf=hf, bi=bi, col=col: t.matmul(
                        [psA, psB][hf][:, :], wg_st[bi][:, kc * 128:(kc + 1) * 128],
                        x3t[:, col:col + TS], start=(kc == 0), stop=(kc == KC - 1)))
                    E.op("tensor", lambda t, kc=kc, hf=hf, bi=bi, col=col: t.matmul(
                        [psC, psD][hf][:, :], wu_st[bi][:, kc * 128:(kc + 1) * 128],
                        x3t[:, col:col + TS], start=(kc == 0), stop=(kc == KC - 1)),
                        inc=(pe if (kc == KC - 1 and hf == 1) else None))
            E.buf_set(("wgu", bi), pe)
            if ic == IC - 1:
                E.buf_set("x3t", pe)
            E.wait("scalar", pe)
            E.buf_wait("scalar", "sg")
            def _sil(s):
                s.activation(out=sg_st[:, 0:TS], in_=psA[:, :], func=AF.Silu)
                s.activation(out=sg_st[:, TS:2 * TS], in_=psB[:, :], func=AF.Silu)
                return s.drain()
            E.op("scalar", _sil, inc=sc)
            ps_set(psA, sc)
            ps_set(psB, sc)
            E.wait("vector", sc, sc.v)
            E.buf_wait("vector", "h_sb")
            def _hm(v, ic=ic):
                v.tensor_tensor(h_sb[:, ic * STW:ic * STW + TS], sg_st[:, 0:TS], psC[:, :], ALU.mult)
                v.drain()
                v.tensor_tensor(h_sb[:, ic * STW + TS:(ic + 1) * STW], sg_st[:, TS:2 * TS],
                                psD[:, :], ALU.mult)
                return v.drain()
            E.op("vector", _hm, inc=ve)
            ps_set(psC, ve)
            ps_set(psD, ve)
            E.buf_set("sg", ve)
        h_ready = ve.v
        if debug is not None and debug[0] == "h" and st == 0:
            E.wait("gpsimd", ve, ve.v)
            E.dma("gpsimd", dbg[:, :], h_sb[:, 0:debug[1][1]], "g_dbg")
        for oc in range(KC):
            bi = oc % 2
            E.buf_wait("sync", ("wd", bi))
            for j in range(IC):
                seg, loc = _down_owner(j)
                doff = seg * SEGSZ + _EG + _EU + (oc * 6 + loc) * 16384
                E.dma("sync", wd_st[bi][:, j * 128:(j + 1) * 128],
                      we_out[doff // 2048:doff // 2048 + 8, :], f"d_wd{bi}")
            E.buf_wait("sync", ("yf", bi))
            E.dma("sync", yf_st[bi][:, :], ys[oc * 128:(oc + 1) * 128, st * STW:(st + 1) * STW], f"d_yf{bi}")
            ps_wait(psA)
            ps_wait(psB)
            E.wait_s("tensor", f"d_wd{bi}")
            E.wait("tensor", ve, h_ready)
            for j in range(IC):
                for hf in range(2):
                    E.op("tensor", lambda t, j=j, hf=hf, bi=bi: t.matmul(
                        [psA, psB][hf][:, :], wd_st[bi][:, j * 128:(j + 1) * 128],
                        h_sb[:, j * STW + hf * TS:j * STW + (hf + 1) * TS],
                        start=(j == 0), stop=(j == IC - 1)),
                        inc=(pe if (j == IC - 1 and hf == 1) else None))
            E.buf_set(("wd", bi), pe)
            if oc == KC - 1:
                E.buf_set("h_sb", pe)
            E.wait("vector", pe)
            E.wait_s("vector", f"d_yf{bi}")
            E.buf_wait("vector", ("of", bi))
            def _dn(v, bi=bi):
                v.tensor_tensor(of_st[bi][:, 0:TS], psA[:, :], yf_st[bi][:, 0:TS], ALU.add)
                v.drain()
                v.tensor_tensor(of_st[bi][:, TS:2 * TS], psB[:, :], yf_st[bi][:, TS:2 * TS], ALU.add)
                return v.drain()
            E.op("vector", _dn, inc=ve)
            ps_set(psA, ve)
            ps_set(psB, ve)
            E.buf_set(("yf", bi), ve)
            E.wait("scalar", ve, ve.v)
            E.dma("scalar", out_ext[oc * 128:(oc + 1) * 128, st * STW:(st + 1) * STW],
                  of_st[bi][:, :], f"g_out{bi}")
            E.buf_set(("of", bi), f"g_out{bi}")

    if debug is not None:
        E.wait_s("gpsimd", "g_dbg")
    E.barrier(esem)

    with nc.Block() as block:
        @block.sync
        def _(e):
            for fn in E.q["sync"]:
                fn(e)

        @block.scalar
        def _(e):
            for fn in E.q["scalar"]:
                fn(e)

        @block.vector
        def _(e):
            for fn in E.q["vector"]:
                fn(e)

        @block.tensor
        def _(e):
            for fn in E.q["tensor"]:
                fn(e)

        @block.gpsimd
        def _(e):
            for fn in E.q["gpsimd"]:
                fn(e)

    l5.close()
    est.close()
    return nc


def _prep_inputs(inputs):
    x = np.asarray(inputs["L0_attn_hidden_states"])[0]
    slope = np.asarray(inputs["L0_attn_slope_rate"])
    w_qkv = np.asarray(inputs["L0_attn_w_qkv"])
    w_gate = np.asarray(inputs["L0_attn_w_gate"])
    w_out = np.asarray(inputs["L0_attn_w_out"])
    norm_w = np.asarray(inputs["L0_attn_norm_w"])
    lut = np.asarray(inputs["L0_ffn_lut"])
    gidx = np.asarray(inputs["L0_ffn_gate_idx"])
    uidx = np.asarray(inputs["L0_ffn_up_idx"])
    didx = np.asarray(inputs["L0_ffn_down_idx"])

    def chunks(wT, noc):
        return np.ascontiguousarray(
            wT.reshape(KC, 128, noc, 128).transpose(0, 2, 1, 3)).reshape(KC * noc * 128, 128)

    wqkv_c = chunks(np.ascontiguousarray(w_qkv.T), OC_QKV)
    wgate_c = chunks(np.ascontiguousarray(w_gate.T), KC)
    wout_c = chunks(np.ascontiguousarray(w_out.T), KC)
    normw_t = np.ascontiguousarray(norm_w.reshape(KC, 128).T)
    gidxT = np.ascontiguousarray(gidx.T)
    uidxT = np.ascontiguousarray(uidx.T)
    didxT = np.ascontiguousarray(didx.T)

    in_maps = []
    for c in range(NC):
        rows = np.zeros((1, 48), np.float32)
        rows[0, 0:16] = slope
        rows[0, 16:32] = lut
        for cp in range(NC):
            rows[0, 32 + cp] = float(c - 1 - cp) if cp < c else 0.0
            rows[0, 40 + cp] = 1.0 if cp < c else 0.0
        dsl = np.zeros((768, D), np.int32)
        r0 = _DSTART[c] * 128
        nreal = _DCHUNKS[c] * 128
        dsl[:nreal] = didxT[r0:r0 + nreal]
        m = {
            "x": np.ascontiguousarray(x[c * T:(c + 1) * T].T),
            "wqkv": wqkv_c,
            "wgate": wgate_c,
            "wout": wout_c,
            "normw": normw_t,
            "rows": rows,
            "gidx": np.ascontiguousarray(gidxT[c * 256:(c + 1) * 256]),
            "uidx": np.ascontiguousarray(uidxT[c * 256:(c + 1) * 256]),
            "didx": dsl,
        }
        in_maps.append(m)
    return in_maps


def _kernel_impl(inputs, debug=None):
    in_maps = _prep_inputs(inputs)
    nc = build_program(debug=debug)
    res = run_bass_kernel_spmd(nc, in_maps, core_ids=list(range(NC)))
    outs = [r["out"] for r in res.results]
    full = np.empty((1, SEQ, D), np.float32)
    for c in range(NC):
        full[0, c * T:(c + 1) * T, :] = outs[c].T
    if debug is not None:
        return full, [r["dbg"] for r in res.results]
    return full


def kernel(**inputs):
    return _kernel_impl(inputs)


# version 04
# speedup vs baseline: 1.2767x; 1.0643x over previous
from contextlib import ExitStack

import numpy as np
import ml_dtypes

import concourse.bass as bass
import concourse.mybir as mybir
from concourse.bass_utils import run_bass_kernel_spmd

F32 = mybir.dt.float32
BF16 = mybir.dt.bfloat16
I32 = mybir.dt.int32
AF = mybir.ActivationFunctionType
ALU = mybir.AluOpType
BF = ml_dtypes.bfloat16

NC = 8
SEQ = 16384
D = 2048
T = SEQ // NC
KC = D // 128
H = 16
DH = 128
BLK = 256
NB = T // BLK
OC_QKV = 48
INTER = 5504
IC = INTER // 128
TS = 512
NTS = T // TS

_DCHUNKS = [6, 6, 6, 5, 5, 5, 5, 5]
_DSTART = [0]
for _n in _DCHUNKS[:-1]:
    _DSTART.append(_DSTART[-1] + _n)

_EG = 2 * IC * 128 * 128
_EU = _EG
_ED = 6 * 16 * 128 * 128
SEGSZ = _EG + _EU + _ED

ENGS = ("sync", "scalar", "vector", "tensor", "gpsimd")


def _down_owner(j):
    for c in range(NC):
        if j < _DSTART[c] + _DCHUNKS[c]:
            return c, j - _DSTART[c]
    raise AssertionError


class Ctr:
    def __init__(self, h):
        self.h = h
        self.v = 0


class Emit:
    def __init__(self, nc, est):
        self.q = {e: [] for e in ENGS}
        self.bufs = {}
        self.nc = nc
        self.est = est
        self.streams = {}
        self.stream_eng = {}

    def sem(self, name):
        if name not in self.streams:
            self.streams[name] = Ctr(self.est.enter_context(self.nc.semaphore(name)))
        return self.streams[name]

    def op(self, eng, fn, inc=None, n=1):
        if inc is not None:
            inc.v += n
        def run(e, fn=fn, inc=inc, n=n):
            instr = fn(e)
            if inc is not None:
                instr.then_inc(inc.h, n)
        self.q[eng].append(run)

    def dma(self, eng, out, in_, stream):
        c = self.sem(stream)
        c.v += 16
        self.stream_eng[stream] = eng
        def run(e, out=out, in_=in_, c=c):
            e.dma_start(out=out, in_=in_).then_inc(c.h, 16)
        self.q[eng].append(run)

    def wait(self, eng, ctr, v=None):
        v = ctr.v if v is None else v
        def run(e, h=ctr.h, v=v):
            e.wait_ge(h, v)
        self.q[eng].append(run)

    def wait_s(self, eng, stream):
        self.wait(eng, self.sem(stream))

    def drain(self, eng, inc=None, n=1):
        if inc is not None:
            inc.v += n
        def run(e, inc=inc, n=n):
            instr = e.drain()
            if inc is not None:
                instr.then_inc(inc.h, n)
        self.q[eng].append(run)

    def barrier(self, esem, exclude=()):
        engs = [e for e in ENGS if e not in exclude]
        for name, c in self.streams.items():
            eng = self.stream_eng.get(name, "sync")
            if eng in exclude:
                eng = "sync"
            self.wait(eng, c)
        for eng in engs:
            c = esem[eng]
            c.v += 1
            def run(e, c=c):
                e.sem_inc(c.h, 1)
            self.q[eng].append(run)
        snap = {eng: esem[eng].v for eng in engs}
        for eng in engs:
            for other in engs:
                if other == eng:
                    continue
                self.wait(eng, esem[other], snap[other])

    def buf_wait(self, eng, key):
        for (ctr, v) in self.bufs.get(key, ()):
            self.wait(eng, ctr, v)

    def buf_set(self, key, *specs):
        out = []
        for s in specs:
            if isinstance(s, str):
                c = self.sem(s)
                out.append((c, c.v))
            else:
                out.append((s, s.v))
        self.bufs[key] = out


def build_program(debug=None):
    nc = bass.Bass("TRN2", target_bir_lowering=False, debug=False)

    x_in = nc.dram_tensor("x", [D, T], F32, kind="ExternalInput").ap()
    wqkv_in = nc.dram_tensor("wqkv", [KC * OC_QKV * 128, 128], F32, kind="ExternalInput").ap()
    wgate_in = nc.dram_tensor("wgate", [KC * KC * 128, 128], F32, kind="ExternalInput").ap()
    wout_in = nc.dram_tensor("wout", [KC * KC * 128, 128], F32, kind="ExternalInput").ap()
    normw_in = nc.dram_tensor("normw", [128, KC], F32, kind="ExternalInput").ap()
    rows_in = nc.dram_tensor("rows", [1, 48], F32, kind="ExternalInput").ap()
    gidx_in = nc.dram_tensor("gidx", [2 * 128, INTER], I32, kind="ExternalInput").ap()
    uidx_in = nc.dram_tensor("uidx", [2 * 128, INTER], I32, kind="ExternalInput").ap()
    didx_in = nc.dram_tensor("didx", [6 * 128, D], I32, kind="ExternalInput").ap()
    out_ext = nc.dram_tensor("out", [D, T], F32, kind="ExternalOutput").ap()

    qs = nc.dram_tensor("qs", [H * 128, T], BF16).ap()
    ks = nc.dram_tensor("ks", [H * 128, T], BF16).ap()
    vts = nc.dram_tensor("vts", [H * NB * 2 * 128, 128], BF16).ap()
    ls = nc.dram_tensor("ls", [H * 128, NB * 128], F32).ap()
    gs = nc.dram_tensor("gs", [D, T], BF16).ap()
    as_s = nc.dram_tensor("as_s", [D, T], BF16).ap()
    ys = nc.dram_tensor("ys", [D, T], F32).ap()
    x3s = nc.dram_tensor("x3s", [D, T], BF16).ap()
    wouts = nc.dram_tensor("wouts", [KC * 128, KC * 128], BF16).ap()
    cc_in = nc.dram_tensor("cc_in", [128, H * 128], F32).ap()
    cc_out = nc.dram_tensor("cc_out", [NC * 128, H * 128], F32).ap()
    we_in = nc.dram_tensor("we_in", [SEGSZ // 2048, 2048], BF16).ap()
    we_out = nc.dram_tensor("we_out", [NC * SEGSZ // 2048, 2048], BF16).ap()

    dbg = None
    if debug is not None:
        dbg = nc.dram_tensor("dbg", list(debug[1]), debug[2], kind="ExternalOutput").ap()

    est = ExitStack()
    E = Emit(nc, est)

    pe = Ctr(est.enter_context(nc.semaphore("pe")))
    sc = Ctr(est.enter_context(nc.semaphore("sc")))
    ve = Ctr(est.enter_context(nc.semaphore("ve")))
    ccs1 = Ctr(est.enter_context(nc.semaphore("ccs1")))
    ccs2 = Ctr(est.enter_context(nc.semaphore("ccs2")))
    esem = {e: Ctr(est.enter_context(nc.semaphore("b_" + e))) for e in ENGS}

    ident = est.enter_context(nc.sbuf_tensor("ident", [128, 128], BF16))
    ones_k = est.enter_context(nc.sbuf_tensor("ones_k", [128, 1], BF16))
    ones_r = est.enter_context(nc.sbuf_tensor("ones_r", [1, 128], F32))
    consts = est.enter_context(nc.sbuf_tensor("consts", [128, 8], F32))
    rowst = est.enter_context(nc.sbuf_tensor("rowst", [1, 48], F32))
    brc = est.enter_context(nc.sbuf_tensor("brc", [128, 96], F32))
    normw_t = est.enter_context(nc.sbuf_tensor("normw_t", [128, KC], F32))
    powD = est.enter_context(nc.sbuf_tensor("powD", [128, NB * H], F32))
    cfac = est.enter_context(nc.sbuf_tensor("cfac", [128, NC * H], F32))
    iota_j = est.enter_context(nc.sbuf_tensor("iota_j", [128, BLK], F32))
    srow = est.enter_context(nc.sbuf_tensor("srow", [1, TS], F32))
    sqrow = est.enter_context(nc.sbuf_tensor("sqrow", [1, TS], F32))
    arow = est.enter_context(nc.sbuf_tensor("arow", [1, T], F32))
    brow = est.enter_context(nc.sbuf_tensor("brow", [1, T], F32))
    rrow = est.enter_context(nc.sbuf_tensor("rrow", [1, T], F32))
    lmid = ExitStack()
    iota_nm = lmid.enter_context(nc.sbuf_tensor("iota_nm", [128, 512], F32))
    mskc = lmid.enter_context(nc.sbuf_tensor("mskc", [128, 512], BF16))
    bc1 = lmid.enter_context(nc.sbuf_tensor("bc1", [128, TS], F32))
    bc2 = lmid.enter_context(nc.sbuf_tensor("bc2", [128, TS], F32))

    psA = est.enter_context(nc.psum_tensor("psA", [128, TS], F32))
    psB = est.enter_context(nc.psum_tensor("psB", [128, TS], F32))
    psC = est.enter_context(nc.psum_tensor("psC", [128, TS], F32))
    psD = est.enter_context(nc.psum_tensor("psD", [128, TS], F32))
    lps = ExitStack()
    psT = lps.enter_context(nc.psum_tensor("psT", [128, 256], BF16))
    psT2 = lps.enter_context(nc.psum_tensor("psT2", [128, 256], BF16))
    psR = lps.enter_context(nc.psum_tensor("psR", [1, TS], F32))
    psAT = lps.enter_context(nc.psum_tensor("psAT", [128, TS], F32))

    psum4 = [psA, psB, psC, psD]
    ps_free = {}

    def ps_wait(p):
        f = ps_free.get(id(p))
        if f is not None:
            E.wait("tensor", f[0], f[1])

    def ps_set(p, ctr):
        ps_free[id(p)] = (ctr, ctr.v)

    E.dma("sync", rowst[:, :], rows_in[:, :], "d_misc")
    E.dma("sync", normw_t[:, :], normw_in[:, :], "d_misc")

    def _g0(g):
        g.memset(ident[:], 0.0)
        g.drain()
        g.affine_select(out=ident[:], in_=ident[:], compare_op=ALU.not_equal,
                        fill=1.0, base=0, pattern=[[-1, 128]], channel_multiplier=1)
        g.iota(iota_j[:], pattern=[[1, BLK]], base=0, channel_multiplier=0,
               allow_small_or_imprecise_dtypes=True)
        g.iota(iota_nm[:, 0:256], pattern=[[1, 256]], base=0, channel_multiplier=-1,
               allow_small_or_imprecise_dtypes=True)
        g.iota(iota_nm[:, 256:512], pattern=[[1, 256]], base=-128, channel_multiplier=-1,
               allow_small_or_imprecise_dtypes=True)
        return g.drain()
    E.op("gpsimd", _g0, inc=esem["gpsimd"])

    def _v0(v):
        v.memset(ones_k[:], 1.0)
        v.memset(ones_r[:], 1.0)
        v.memset(consts[:, 0:1], 1e-5)
        v.memset(consts[:, 1:2], 1e-6)
        v.memset(rrow[0:1, :], 0.0)
        return v.drain()
    E.op("vector", _v0, inc=ve)
    E.wait("vector", esem["gpsimd"])
    E.wait_s("vector", "d_misc")
    E.op("vector", lambda v: v.tensor_scalar(mskc[:, :], iota_nm[:, :], 0.0, 1.0, ALU.is_ge, ALU.mult))
    E.drain("vector", inc=ve)

    E.wait("tensor", ve, ve.v)
    E.wait_s("tensor", "d_misc")
    E.op("tensor", lambda t: t.matmul(psC[0:128, 0:48], ones_r[:, :], rowst[:, :],
                                      start=True, stop=True), inc=pe)
    E.wait("vector", pe)
    E.op("vector", lambda v: v.tensor_copy(brc[:, 0:48], psC[0:128, 0:48]))
    E.drain("vector", inc=ve)
    ps_set(psC, ve)
    E.op("vector", lambda v: v.tensor_scalar(brc[:, 48:64], brc[:, 0:16], -1.0, None, ALU.mult))
    E.op("vector", lambda v: v.tensor_scalar(brc[:, 64:80], brc[:, 0:16], -255.0, None, ALU.mult))
    E.drain("vector", inc=ve)

    def _vpow(v):
        for b in range(NB):
            v.tensor_scalar(powD[:, b * 16:(b + 1) * 16], brc[:, 0:16],
                            -256.0 * b, None, ALU.mult)
        for cp in range(NC):
            v.tensor_scalar(cfac[:, cp * 16:(cp + 1) * 16], brc[:, 0:16],
                            brc[:, 32 + cp:33 + cp], -2048.0, ALU.mult, ALU.mult)
        return v.drain()
    E.op("vector", _vpow, inc=ve)
    E.wait("scalar", ve, ve.v)
    E.op("scalar", lambda s: s.activation(out=powD[:, :], in_=powD[:, :], func=AF.Exp))
    E.op("scalar", lambda s: s.activation(out=cfac[:, :], in_=cfac[:, :], func=AF.Exp))
    E.drain("scalar", inc=sc)
    E.wait("vector", sc, sc.v)
    def _vmask(v):
        for cp in range(NC):
            v.tensor_scalar(cfac[:, cp * 16:(cp + 1) * 16], cfac[:, cp * 16:(cp + 1) * 16],
                            brc[:, 40 + cp:41 + cp], None, ALU.mult)
        return v.drain()
    E.op("vector", _vmask, inc=ve)

    E.barrier(esem)

    l2 = ExitStack()
    x2 = l2.enter_context(nc.sbuf_tensor("x2", [128, KC * T], BF16))
    kdec = l2.enter_context(nc.sbuf_tensor("kdec", [128, H * BLK], BF16))
    xst = [l2.enter_context(nc.sbuf_tensor(f"xst_{i}", [128, TS], F32)) for i in range(2)]
    xbf = [l2.enter_context(nc.sbuf_tensor(f"xbf_{i}", [128, TS], BF16)) for i in range(2)]
    xsq = [l2.enter_context(nc.sbuf_tensor(f"xsq_{i}", [128, TS], BF16)) for i in range(2)]
    NW = 4
    wf32 = [l2.enter_context(nc.sbuf_tensor(f"wf32_{i}", [128, KC * 128], F32)) for i in range(NW)]
    wbf = [l2.enter_context(nc.sbuf_tensor(f"wbf_{i}", [128, KC * 128], BF16)) for i in range(NW)]
    qkvt = [l2.enter_context(nc.sbuf_tensor(f"qkvt_{i}", [128, T], BF16)) for i in range(3)]
    kdt = l2.enter_context(nc.sbuf_tensor("kdt", [128, BLK], BF16))
    tst = l2.enter_context(nc.sbuf_tensor("tst", [128, 512], BF16))
    ctile = [l2.enter_context(nc.sbuf_tensor(f"ctile_{i}", [128, 128], F32)) for i in range(2)]
    L_all = l2.enter_context(nc.sbuf_tensor("L_all", [128, (NB + 1) * 128], F32))
    EW = 1408
    ibuf = [l2.enter_context(nc.sbuf_tensor(f"ibuf_{i}", [128, EW], I32)) for i in range(2)]
    ibf = l2.enter_context(nc.sbuf_tensor("ibf", [128, EW], BF16))
    eacc = l2.enter_context(nc.sbuf_tensor("eacc", [128, EW], BF16))
    epl = l2.enter_context(nc.sbuf_tensor("epl", [128, EW], BF16))

    def _kdecgen(s):
        for h in range(H):
            s.activation(out=kdec[:, h * BLK:(h + 1) * BLK], in_=iota_j[:, :],
                         func=AF.Exp, bias=brc[:, 64 + h:65 + h], scale=brc[:, h:h + 1])
        return s.drain()
    E.op("scalar", _kdecgen, inc=sc)

    x2_ts = []
    for ts in range(NTS):
        for kc in range(KC):
            bi = kc % 2
            E.buf_wait("gpsimd", ("xst", bi))
            E.dma("gpsimd", xst[bi][:, :], x_in[kc * 128:(kc + 1) * 128, ts * TS:(ts + 1) * TS], f"d_x{bi}")
            E.wait_s("vector", f"d_x{bi}")
            E.buf_wait("vector", ("xbf", bi))
            E.op("vector", lambda v, bi=bi: v.tensor_copy(xbf[bi][:, :], xst[bi][:, :]))
            E.drain("vector", inc=ve)
            E.buf_set(("xst", bi), ve)
            E.wait("scalar", ve, ve.v)
            E.buf_wait("scalar", ("xsq", bi))
            E.op("scalar", lambda s, bi=bi: s.activation(out=xsq[bi][:, :], in_=xbf[bi][:, :], func=AF.Square))
            E.drain("scalar", inc=sc)
            if kc == 0:
                ps_wait(psR)
                ps_wait(psAT)
            E.wait("tensor", ve, ve.v)
            E.wait("tensor", sc, sc.v)
            E.op("tensor", lambda t, kc=kc, bi=bi: t.matmul(psR[0:1, 0:TS], ones_k[:, :], xbf[bi][:, :],
                                                            start=(kc == 0), stop=(kc == KC - 1)))
            E.op("tensor", lambda t, kc=kc, bi=bi: t.matmul(psAT[0:1, 0:TS], ones_k[:, :], xsq[bi][:, :],
                                                            start=(kc == 0), stop=(kc == KC - 1)), inc=pe)
            E.buf_set(("xbf", bi), pe)
            E.buf_set(("xsq", bi), pe)
        E.wait("vector", pe)
        E.buf_wait("vector", "statrows")
        def _rows(v, ts=ts):
            sl = slice(ts * TS, (ts + 1) * TS)
            v.tensor_scalar(srow[0:1, :], psR[0:1, 0:TS], 1.0 / D, None, ALU.mult)
            v.tensor_scalar(sqrow[0:1, :], psAT[0:1, 0:TS], 1.0 / D, None, ALU.mult)
            v.drain()
            v.tensor_tensor(arow[0:1, sl], srow[0:1, :], srow[0:1, :], ALU.mult)
            v.drain()
            v.tensor_tensor(arow[0:1, sl], sqrow[0:1, :], arow[0:1, sl], ALU.subtract)
            return v.drain()
        E.op("vector", _rows, inc=ve)
        ps_set(psR, ve)
        ps_set(psAT, ve)
        E.wait("scalar", ve, ve.v)
        E.op("scalar", lambda s, ts=ts: s.activation(out=arow[0:1, ts * TS:(ts + 1) * TS],
                                                     in_=arow[0:1, ts * TS:(ts + 1) * TS],
                                                     func=AF.Sqrt, bias=consts[0:1, 0:1]))
        E.drain("scalar", inc=sc)
        E.wait("vector", sc, sc.v)
        def _rows2(v, ts=ts):
            sl = slice(ts * TS, (ts + 1) * TS)
            v.reciprocal(arow[0:1, sl], arow[0:1, sl])
            v.drain()
            v.tensor_tensor(brow[0:1, sl], srow[0:1, :], arow[0:1, sl], ALU.mult)
            return v.drain()
        E.op("vector", _rows2, inc=ve)
        E.buf_set("statrows", ve)
        ps_wait(psC)
        ps_wait(psD)
        E.wait("tensor", ve, ve.v)
        E.op("tensor", lambda t, ts=ts: t.matmul(psC[0:128, 0:TS], ones_r[:, :],
                                                 arow[0:1, ts * TS:(ts + 1) * TS], start=True, stop=True))
        E.op("tensor", lambda t, ts=ts: t.matmul(psD[0:128, 0:TS], ones_r[:, :],
                                                 brow[0:1, ts * TS:(ts + 1) * TS], start=True, stop=True),
             inc=pe)
        E.wait("vector", pe)
        E.buf_wait("vector", "bc12")
        def _cpbc(v):
            v.tensor_copy(bc1[:, :], psC[0:128, 0:TS])
            v.tensor_copy(bc2[:, :], psD[0:128, 0:TS])
            return v.drain()
        E.op("vector", _cpbc, inc=ve)
        ps_set(psC, ve)
        ps_set(psD, ve)
        for kc in range(KC):
            bi = kc % 2
            E.buf_wait("gpsimd", ("xst", bi))
            E.dma("gpsimd", xst[bi][:, :], x_in[kc * 128:(kc + 1) * 128, ts * TS:(ts + 1) * TS], f"d_x{bi}")
            E.wait_s("vector", f"d_x{bi}")
            def _norm(v, kc=kc, ts=ts, bi=bi):
                v.tensor_tensor(xst[bi][:, :], xst[bi][:, :], bc1[:, :], ALU.mult)
                v.drain()
                v.tensor_tensor(x2[:, kc * T + ts * TS:kc * T + (ts + 1) * TS],
                                xst[bi][:, :], bc2[:, :], ALU.subtract)
                return v.drain()
            E.op("vector", _norm, inc=ve)
            E.buf_set(("xst", bi), ve)
        E.buf_set("bc12", ve)
        x2_ts.append(ve.v)
    x2_ready = ve.v

    if debug is not None and debug[0] == "x2":
        E.wait("gpsimd", ve, ve.v)
        for kc in range(KC):
            E.dma("gpsimd", dbg[kc * 128:(kc + 1) * 128, :], x2[:, kc * T:(kc + 1) * T], "g_dbg")

    wqkv_v = wqkv_in.rearrange("(kc o p) j -> o p kc j", o=OC_QKV, p=128)
    wgate_v = wgate_in.rearrange("(kc o p) j -> o p kc j", o=KC, p=128)
    wout_v = wout_in.rearrange("(kc o p) j -> o p kc j", o=KC, p=128)

    wslot = [0]

    def load_w(w_v, oc):
        wbuf_i = wslot[0] % NW
        wslot[0] += 1
        E.buf_wait("sync", ("wf32", wbuf_i))
        E.dma("sync", wf32[wbuf_i][:, :], w_v[oc:oc + 1], f"d_w{wbuf_i}")
        E.wait_s("scalar", f"d_w{wbuf_i}")
        E.buf_wait("scalar", ("wbf", wbuf_i))
        E.op("scalar", lambda s: s.activation(out=wbf[wbuf_i][:, :], in_=wf32[wbuf_i][:, :],
                                              func=AF.Copy))
        E.drain("scalar", inc=sc)
        E.buf_set(("wf32", wbuf_i), sc)
        return wbuf_i

    def project_oc(oc, w_v, dst, actfunc):
        wbuf_i = load_w(w_v, oc)
        sc_w = sc.v
        for ts in range(NTS):
            p = psum4[ts]
            ps_wait(p)
            E.wait("tensor", sc, sc_w)
            E.wait("tensor", ve, x2_ts[ts])
            for kc in range(KC):
                E.op("tensor", lambda t, p=p, kc=kc, ts=ts, wbuf_i=wbuf_i: t.matmul(
                    p[:, :], wbf[wbuf_i][:, kc * 128:(kc + 1) * 128],
                    x2[:, kc * T + ts * TS:kc * T + (ts + 1) * TS],
                    start=(kc == 0), stop=(kc == KC - 1)),
                    inc=(pe if kc == KC - 1 else None))
            if ts == NTS - 1:
                E.buf_set(("wbf", wbuf_i), pe)
            E.wait("scalar", pe)
            E.op("scalar", lambda s, p=p, ts=ts: s.activation(
                out=dst[:, ts * TS:(ts + 1) * TS], in_=p[:, :], func=actfunc))
            E.drain("scalar", inc=sc)
            ps_set(p, sc)

    exp_tiles = []
    for krow in range(2):
        exp_tiles.append(("g", gidx_in, krow, INTER))
    for krow in range(2):
        exp_tiles.append(("u", uidx_in, krow, INTER))
    for loc in range(6):
        exp_tiles.append(("d", didx_in, loc, D))

    exp_slices = []
    for (mat, src_t, srow_i, W) in exp_tiles:
        cbase = 0
        while cbase < W:
            w = min(EW, W - cbase)
            exp_slices.append((mat, src_t, srow_i, cbase, w))

            cbase += w

    exp_ctr = [0]

    def emit_exp_slice(spec):
        (mat, src_t, srow_i, cbase, w) = spec
        bi = exp_ctr[0] % 2
        exp_ctr[0] += 1
        E.buf_wait("sync", ("ibuf", bi))
        E.dma("sync", ibuf[bi][:, 0:w],
              src_t[srow_i * 128:(srow_i + 1) * 128, cbase:cbase + w], f"d_ib{bi}")
        E.wait_s("vector", f"d_ib{bi}")
        E.buf_wait("vector", "eacc")
        def _expand(v, w=w, bi=bi):
            v.tensor_copy(ibf[:, 0:w], ibuf[bi][:, 0:w])
            v.drain()
            v.tensor_scalar(eacc[:, 0:w], ibf[:, 0:w], 0.0, brc[:, 16:17], ALU.is_equal, ALU.mult)
            for l in range(1, 16):
                v.drain()
                v.tensor_scalar(epl[:, 0:w], ibf[:, 0:w], float(l), brc[:, 16 + l:17 + l],
                                ALU.is_equal, ALU.mult)
                v.drain()
                v.tensor_tensor(eacc[:, 0:w], eacc[:, 0:w], epl[:, 0:w], ALU.add)
            return v.drain()
        E.op("vector", _expand, inc=ve)
        E.buf_set(("ibuf", bi), ve)
        E.wait("gpsimd", ve, ve.v)
        for ci in range(w // 128):
            c = cbase // 128 + ci
            if mat == "g":
                off = (c * 2 + srow_i) * 16384
            elif mat == "u":
                off = _EG + (c * 2 + srow_i) * 16384
            else:
                off = _EG + _EU + (c * 6 + srow_i) * 16384
            r = off // 2048
            E.dma("gpsimd", we_in[r:r + 8, :], eacc[:, ci * 128:(ci + 1) * 128], "g_we")
        E.buf_set("eacc", "g_we")

    exp_iter = iter(exp_slices)

    for h in range(H):
        E.buf_wait("scalar", ("qkvt", 1))
        project_oc(16 + h, wqkv_v, qkvt[1], AF.Silu)
        sc_k = sc.v
        E.buf_wait("scalar", ("qkvt", 2))
        project_oc(32 + h, wqkv_v, qkvt[2], AF.Silu)
        sc_v = sc.v
        E.buf_wait("scalar", ("qkvt", 0))
        project_oc(h, wqkv_v, qkvt[0], AF.Silu)
        sc_q = sc.v
        E.wait("gpsimd", sc, sc_k)
        E.dma("gpsimd", ks[h * 128:(h + 1) * 128, :], qkvt[1][:, :], "g_k")
        E.wait("gpsimd", sc, sc_q)
        E.dma("gpsimd", qs[h * 128:(h + 1) * 128, :], qkvt[0][:, :], "g_q")
        E.buf_set(("qkvt", 0), "g_q")
        E.buf_wait("vector", "L_all")
        E.op("vector", lambda v: v.memset(L_all[:, 0:128], 0.0))
        E.drain("vector", inc=ve)
        for b in range(NB):
            E.wait("vector", sc, sc_k)
            E.buf_wait("vector", "kdt")
            E.op("vector", lambda v, h=h, b=b: v.tensor_tensor(
                kdt[:, :], qkvt[1][:, b * BLK:(b + 1) * BLK], kdec[:, h * BLK:(h + 1) * BLK], ALU.mult))
            E.drain("vector", inc=ve)
            ps_wait(psT)
            ps_wait(psT2)
            E.buf_wait("tensor", "tst_w")
            E.wait("tensor", ve, ve.v)
            E.wait("tensor", sc, sc_v)
            for j in range(2):
                E.op("tensor", lambda t, j=j: t.transpose(
                    psT[:, j * 128:(j + 1) * 128], kdt[:, j * 128:(j + 1) * 128], ident[:]))
                E.op("tensor", lambda t, j=j, b=b: t.transpose(
                    psT2[:, j * 128:(j + 1) * 128],
                    qkvt[2][:, b * BLK + j * 128:b * BLK + (j + 1) * 128], ident[:]),
                    inc=(pe if j == 1 else None))
            E.buf_set("kdt", pe)
            E.wait("scalar", pe)
            E.buf_wait("scalar", "tst_w")
            def _evt(s):
                s.activation(out=tst[:, 0:256], in_=psT[:, :], func=AF.Copy)
                s.activation(out=tst[:, 256:512], in_=psT2[:, :], func=AF.Copy)
                return s.drain()
            E.op("scalar", _evt, inc=sc)
            ps_set(psT, sc)
            ps_set(psT2, sc)
            E.wait("gpsimd", sc, sc.v)
            for j in range(2):
                E.dma("gpsimd",
                      vts[(h * NB * 2 + b * 2 + j) * 128:(h * NB * 2 + b * 2 + j + 1) * 128, :],
                      tst[:, 256 + j * 128:256 + (j + 1) * 128], "g_vt")
            half = b % 2
            pcol = slice(half * 128, half * 128 + 128)
            ps_wait(psAT)
            E.wait("tensor", sc, sc.v)
            for j in range(2):
                E.op("tensor", lambda t, j=j, pcol=pcol: t.matmul(
                    psAT[:, pcol], tst[:, j * 128:(j + 1) * 128],
                    tst[:, 256 + j * 128:256 + (j + 1) * 128],
                    start=(j == 0), stop=(j == 1)), inc=(pe if j == 1 else None))
            E.buf_set("tst_w", pe, "g_vt")
            E.wait("scalar", pe)
            E.buf_wait("scalar", ("ctile", half))
            E.op("scalar", lambda s, half=half, pcol=pcol: s.activation(
                out=ctile[half][:, :], in_=psAT[:, pcol], func=AF.Copy))
            E.drain("scalar", inc=sc)
            ps_set(psAT, sc)
            E.wait("vector", sc, sc.v)
            E.op("vector", lambda v, b=b, h=h, half=half: v.scalar_tensor_tensor(
                L_all[:, (b + 1) * 128:(b + 2) * 128], L_all[:, b * 128:(b + 1) * 128],
                powD[:, 16 + h:17 + h], ctile[half][:, :], ALU.mult, ALU.add))
            E.drain("vector", inc=ve)
            E.buf_set(("ctile", half), ve)
        E.buf_set(("qkvt", 1), ve, "g_k")
        E.buf_set(("qkvt", 2), pe)
        E.wait("gpsimd", ve, ve.v)
        E.dma("gpsimd", ls[h * 128:(h + 1) * 128, :], L_all[:, 0:NB * 128], "g_l")
        E.dma("gpsimd", cc_in[:, h * 128:(h + 1) * 128], L_all[:, NB * 128:(NB + 1) * 128], "g_l")
        E.buf_set("L_all", "g_l")
        for _ in range(2):
            spec = next(exp_iter, None)
            if spec is not None:
                emit_exp_slice(spec)

    for spec in exp_iter:
        emit_exp_slice(spec)

    E.wait_s("gpsimd", "g_l")
    E.op("gpsimd", lambda g: g.collective_compute(
        "AllGather", ALU.bypass, replica_groups=[list(range(NC))],
        ins=[cc_in], outs=[cc_out]), inc=ccs1)
    E.wait_s("gpsimd", "g_we")
    E.op("gpsimd", lambda g: g.collective_compute(
        "AllGather", ALU.bypass, replica_groups=[list(range(NC))],
        ins=[we_in], outs=[we_out]), inc=ccs2)

    for oc in range(KC):
        gi = oc % 2
        gtile = qkvt[0] if gi == 0 else qkvt[2]
        E.buf_wait("scalar", ("qkvt", 0 if gi == 0 else 2))
        project_oc(oc, wgate_v, gtile, AF.Sigmoid)
        E.wait("vector", sc, sc.v)
        E.op("vector", lambda v, oc=oc, gtile=gtile: v.tensor_scalar(
            gtile[:, :], gtile[:, :], normw_t[:, oc:oc + 1], None, ALU.mult))
        E.drain("vector", inc=ve)
        E.wait("scalar", ve, ve.v)
        E.dma("scalar", gs[oc * 128:(oc + 1) * 128, :], gtile[:, :], f"g_g{gi}")
        E.buf_set(("qkvt", 0 if gi == 0 else 2), f"g_g{gi}")

    for oc in range(KC):
        wbuf_i = load_w(wout_v, oc)
        E.dma("scalar", wouts[oc * 128:(oc + 1) * 128, :], wbf[wbuf_i][:, :], "g_wo")
        E.buf_set(("wbf", wbuf_i), "g_wo")

    if debug is not None and debug[0] == "qkv":
        E.wait_s("gpsimd", "g_q")
        E.wait_s("gpsimd", "g_k")
        E.dma("gpsimd", dbg[0:H * 128, 0:T], qs[:, :], "g_dbg")
        E.dma("gpsimd", dbg[H * 128:2 * H * 128, 0:T], ks[:, :], "g_dbg")

    E.barrier(esem, exclude=("gpsimd",))
    l2.close()

    l3 = ExitStack()
    qdec = l3.enter_context(nc.sbuf_tensor("qdec", [128, H * BLK], BF16))
    ddm = l3.enter_context(nc.sbuf_tensor("ddm", [128, H * 512], BF16))
    ddtmp = l3.enter_context(nc.sbuf_tensor("ddtmp", [128, 512], F32))
    S_t = l3.enter_context(nc.sbuf_tensor("S_t", [128, H * 128], F32))
    cstage = [l3.enter_context(nc.sbuf_tensor(f"cstage_{i}", [128, H * 128], F32)) for i in range(2)]
    qb_t = [l3.enter_context(nc.sbuf_tensor(f"qb_{i}", [128, BLK], BF16)) for i in range(2)]
    kb_t = [l3.enter_context(nc.sbuf_tensor(f"kb_{i}", [128, BLK], BF16)) for i in range(2)]
    vtb_t = [l3.enter_context(nc.sbuf_tensor(f"vtb_{i}", [128, BLK], BF16)) for i in range(2)]
    lb_t = [l3.enter_context(nc.sbuf_tensor(f"lb_{i}", [128, 128], F32)) for i in range(2)]
    kvt_f = l3.enter_context(nc.sbuf_tensor("kvt_f", [128, 128], F32))
    kvt_b = l3.enter_context(nc.sbuf_tensor("kvt_b", [128, 128], BF16))
    qd_t = l3.enter_context(nc.sbuf_tensor("qd_t", [128, BLK], BF16))
    sd_t = l3.enter_context(nc.sbuf_tensor("sd_t", [128, 512], BF16))
    at_t = [l3.enter_context(nc.sbuf_tensor(f"at_{i}", [128, BLK], BF16)) for i in range(2)]
    sq_t = [l3.enter_context(nc.sbuf_tensor(f"sq_{i}", [128, BLK], BF16)) for i in range(2)]

    def _qdecgen(s):
        for h in range(H):
            s.activation(out=qdec[:, h * BLK:(h + 1) * BLK], in_=iota_j[:, :],
                         func=AF.Exp, bias=brc[:, 48 + h:49 + h], scale=brc[:, 48 + h:49 + h])
        return s.drain()
    E.op("scalar", _qdecgen, inc=sc)
    for h in range(H):
        E.buf_wait("scalar", "ddtmp")
        E.op("scalar", lambda s, h=h: s.activation(out=ddtmp[:, :], in_=iota_nm[:, :],
                                                   func=AF.Exp, scale=brc[:, 48 + h:49 + h]))
        E.drain("scalar", inc=sc)
        E.wait("vector", sc, sc.v)
        E.op("vector", lambda v, h=h: v.tensor_tensor(
            ddm[:, h * 512:(h + 1) * 512], ddtmp[:, :], mskc[:, :], ALU.mult))
        E.drain("vector", inc=ve)
        E.buf_set("ddtmp", ve)

    E.op("vector", lambda v: v.memset(S_t[:, :], 0.0))
    E.drain("vector", inc=ve)
    E.wait("sync", ccs1, 1)
    for cp in range(NC):
        bi = cp % 2
        E.buf_wait("sync", ("cstage", bi))
        E.dma("sync", cstage[bi][:, :], cc_out[cp * 128:(cp + 1) * 128, :], f"d_cs{bi}")
        E.wait_s("vector", f"d_cs{bi}")
        def _sacc(v, cp=cp, bi=bi):
            for h in range(H):
                v.scalar_tensor_tensor(S_t[:, h * 128:(h + 1) * 128],
                                       cstage[bi][:, h * 128:(h + 1) * 128],
                                       cfac[:, cp * 16 + h:cp * 16 + h + 1],
                                       S_t[:, h * 128:(h + 1) * 128], ALU.mult, ALU.add)
                v.drain()
            return v.drain()
        E.op("vector", _sacc, inc=ve)
        E.buf_set(("cstage", bi), ve)

    if debug is not None and debug[0] == "S":
        E.wait("gpsimd", ve, ve.v)
        E.dma("gpsimd", dbg[0:128, :], S_t[:, :], "g_dbg")

    for b in range(NB):
        for h in range(H):
            bi = h % 2
            E.buf_wait("sync", ("pb", bi))
            E.dma("sync", qb_t[bi][:, :], qs[h * 128:(h + 1) * 128, b * BLK:(b + 1) * BLK], f"d_pb{bi}")
            E.dma("sync", kb_t[bi][:, :], ks[h * 128:(h + 1) * 128, b * BLK:(b + 1) * BLK], f"d_pb{bi}")
            for j in range(2):
                E.dma("sync", vtb_t[bi][:, j * 128:(j + 1) * 128],
                      vts[(h * NB * 2 + b * 2 + j) * 128:(h * NB * 2 + b * 2 + j + 1) * 128, :], f"d_pb{bi}")
            E.dma("sync", lb_t[bi][:, :], ls[h * 128:(h + 1) * 128, b * 128:(b + 1) * 128], f"d_pb{bi}")
            E.wait_s("vector", f"d_pb{bi}")
            E.buf_wait("vector", "kvt")
            E.buf_wait("vector", "qd")
            def _kvt(v, b=b, h=h, bi=bi):
                v.scalar_tensor_tensor(kvt_f[:, :], S_t[:, h * 128:(h + 1) * 128],
                                       powD[:, b * 16 + h:b * 16 + h + 1], lb_t[bi][:, :],
                                       ALU.mult, ALU.add)
                v.drain()
                v.tensor_copy(kvt_b[:, :], kvt_f[:, :])
                v.drain()
                v.tensor_tensor(qd_t[:, :], qb_t[bi][:, :], qdec[:, h * BLK:(h + 1) * BLK], ALU.mult)
                return v.drain()
            E.op("vector", _kvt, inc=ve)
            ve_kvt = ve.v
            ps_wait(psA)
            E.wait_s("tensor", f"d_pb{bi}")
            for j in range(2):
                E.op("tensor", lambda t, j=j, bi=bi: t.matmul(
                    psA[:, j * 256:(j + 1) * 256], kb_t[bi][:, j * 128:(j + 1) * 128],
                    qb_t[bi][:, :], start=True, stop=True), inc=(pe if j == 1 else None))
            E.wait("vector", pe)
            E.buf_wait("vector", "sd")
            E.op("vector", lambda v, h=h: v.tensor_tensor(
                sd_t[:, :], psA[:, :], ddm[:, h * 512:(h + 1) * 512], ALU.mult))
            E.drain("vector", inc=ve)
            ps_set(psA, ve)
            half = h % 2
            acol = slice(half * 256, half * 256 + 256)
            ps_wait(psAT)
            E.wait("tensor", ve, ve.v)
            E.op("tensor", lambda t, acol=acol: t.matmul(
                psAT[:, acol], kvt_b[:, :], qd_t[:, :], start=True, stop=False))
            for j in range(2):
                E.op("tensor", lambda t, j=j, bi=bi, acol=acol: t.matmul(
                    psAT[:, acol], vtb_t[bi][:, j * 128:(j + 1) * 128],
                    sd_t[:, j * 256:(j + 1) * 256], start=False, stop=(j == 1)),
                    inc=(pe if j == 1 else None))
            E.buf_set("sd", pe)
            E.buf_set(("pb", bi), pe)
            E.buf_set("kvt", pe)
            E.buf_set("qd", pe)
            E.wait("scalar", pe)
            E.buf_wait("scalar", ("at", half))
            def _eva(s, half=half, acol=acol):
                s.activation(out=at_t[half][:, :], in_=psAT[:, acol], func=AF.Copy)
                s.activation(out=sq_t[half][:, :], in_=psAT[:, acol], func=AF.Square)
                return s.drain()
            E.op("scalar", _eva, inc=sc)
            ps_set(psAT, sc)
            if h == 0:
                ps_wait(psR)
            E.wait("tensor", sc, sc.v)
            E.op("tensor", lambda t, h=h, half=half: t.matmul(
                psR[0:1, 0:256], ones_k[:, :], sq_t[half][:, :],
                start=(h == 0), stop=(h == H - 1)), inc=(pe if h == H - 1 else None))
            E.dma("scalar", as_s[h * 128:(h + 1) * 128, b * BLK:(b + 1) * BLK], at_t[half][:, :],
                  f"g_as{half}")
            E.buf_set(("at", half), f"g_as{half}", pe)
        E.wait("vector", pe)
        E.op("vector", lambda v, b=b: v.tensor_copy(rrow[0:1, b * 256:(b + 1) * 256], psR[0:1, 0:256]))
        E.drain("vector", inc=ve)
        ps_set(psR, ve)

    if debug is not None and debug[0] == "attn":
        for kc in range(KC):
            E.dma("gpsimd", dbg[kc * 128:(kc + 1) * 128, :], as_s[kc * 128:(kc + 1) * 128, :], "g_dbg")
        E.barrier(esem)

    E.wait_s("sync", "g_as0")
    E.wait_s("sync", "g_as1")
    E.wait_s("sync", "g_g0")
    E.wait_s("sync", "g_g1")
    E.wait_s("sync", "g_wo")
    E.op("vector", lambda v: v.tensor_scalar(arow[0:1, :], rrow[0:1, :], 1.0 / D, None, ALU.mult))
    E.drain("vector", inc=ve)
    E.wait("scalar", ve, ve.v)
    E.op("scalar", lambda s: s.activation(out=arow[0:1, :], in_=arow[0:1, :],
                                          func=AF.Sqrt, bias=consts[0:1, 1:2]))
    E.drain("scalar", inc=sc)
    E.wait("vector", sc, sc.v)
    E.op("vector", lambda v: v.reciprocal(arow[0:1, :], arow[0:1, :]))
    E.drain("vector", inc=ve)

    gated = [l3.enter_context(nc.sbuf_tensor(f"gated_{i}", [128, TS], BF16)) for i in range(KC)]
    att_st = [l3.enter_context(nc.sbuf_tensor(f"attst_{i}", [128, TS], BF16)) for i in range(2)]
    g_st = [l3.enter_context(nc.sbuf_tensor(f"gst_{i}", [128, TS], BF16)) for i in range(2)]
    xr_st = [l3.enter_context(nc.sbuf_tensor(f"xrst_{i}", [128, TS], F32)) for i in range(2)]
    y_st = [l3.enter_context(nc.sbuf_tensor(f"yst_{i}", [128, TS], F32)) for i in range(2)]
    ybf_st = [l3.enter_context(nc.sbuf_tensor(f"ybfst_{i}", [128, TS], BF16)) for i in range(2)]
    ysq_st = [l3.enter_context(nc.sbuf_tensor(f"ysqst_{i}", [128, TS], BF16)) for i in range(2)]
    wost = [l3.enter_context(nc.sbuf_tensor(f"wost_{i}", [128, KC * 128], BF16)) for i in range(2)]
    x3w = [l3.enter_context(nc.sbuf_tensor(f"x3w_{i}", [128, TS], BF16)) for i in range(2)]

    for ts in range(NTS):
        ps_wait(psC)
        E.wait("tensor", ve, ve.v)
        E.op("tensor", lambda t, ts=ts: t.matmul(psC[0:128, 0:TS], ones_r[:, :],
                                                 arow[0:1, ts * TS:(ts + 1) * TS],
                                                 start=True, stop=True), inc=pe)
        E.wait("vector", pe)
        E.buf_wait("vector", "bc12")
        E.op("vector", lambda v: v.tensor_copy(bc1[:, :], psC[0:128, 0:TS]))
        E.drain("vector", inc=ve)
        ps_set(psC, ve)
        for fc in range(KC):
            bi = fc % 2
            E.buf_wait("sync", ("gats", bi))
            E.dma("sync", att_st[bi][:, :], as_s[fc * 128:(fc + 1) * 128, ts * TS:(ts + 1) * TS], f"d_ga{bi}")
            E.dma("sync", g_st[bi][:, :], gs[fc * 128:(fc + 1) * 128, ts * TS:(ts + 1) * TS], f"d_ga{bi}")
            E.buf_wait("vector", ("xr", bi))
            E.wait_s("vector", f"d_ga{bi}")
            def _gat(v, fc=fc, bi=bi):
                v.tensor_tensor(xr_st[bi][:, :], att_st[bi][:, :], g_st[bi][:, :], ALU.mult)
                v.drain()
                v.tensor_tensor(gated[fc][:, :], xr_st[bi][:, :], bc1[:, :], ALU.mult)
                return v.drain()
            E.op("vector", _gat, inc=ve)
            E.buf_set(("gats", bi), ve)
            E.buf_set(("xr", bi), ve)
        ve_gated = ve.v
        E.buf_set("bc12", ve)
        pend_stat = []

        def emit_stat():
            (oc_, bi_, sc_v_) = pend_stat.pop(0)
            if oc_ == 0:
                ps_wait(psR)
                ps_wait(psAT)
            E.wait("tensor", sc, sc_v_)
            E.op("tensor", lambda t, oc=oc_, bi=bi_: t.matmul(
                psR[0:1, 0:TS], ones_k[:, :], ybf_st[bi][:, :],
                start=(oc == 0), stop=(oc == KC - 1)))
            E.op("tensor", lambda t, oc=oc_, bi=bi_: t.matmul(
                psAT[0:1, 0:TS], ones_k[:, :], ysq_st[bi][:, :],
                start=(oc == 0), stop=(oc == KC - 1)), inc=pe)
            E.buf_set(("ybf", bi_), pe)

        for oc in range(KC):
            bi = oc % 2
            E.buf_wait("sync", ("wost", bi))
            E.dma("sync", wost[bi][:, :], wouts[oc * 128:(oc + 1) * 128, :], f"d_wo{bi}")
            E.buf_wait("sync", ("xr", bi))
            E.dma("sync", xr_st[bi][:, :], x_in[oc * 128:(oc + 1) * 128, ts * TS:(ts + 1) * TS], f"d_xr{bi}")
            p = [psA, psB, psD][oc % 3]
            ps_wait(p)
            E.wait("tensor", ve, ve_gated)
            E.wait_s("tensor", f"d_wo{bi}")
            for fc in range(KC):
                E.op("tensor", lambda t, p=p, fc=fc, bi=bi: t.matmul(
                    p[:, :], wost[bi][:, fc * 128:(fc + 1) * 128],
                    gated[fc][:, :], start=(fc == 0), stop=(fc == KC - 1)),
                    inc=(pe if fc == KC - 1 else None))
            E.buf_set(("wost", bi), pe)
            E.wait("vector", pe)
            E.wait_s("vector", f"d_xr{bi}")
            E.buf_wait("vector", ("y", bi))
            E.op("vector", lambda v, p=p, bi=bi: v.tensor_tensor(
                y_st[bi][:, :], p[:, :], xr_st[bi][:, :], ALU.add))
            E.drain("vector", inc=ve)
            ps_set(p, ve)
            E.buf_set(("xr", bi), ve)
            E.wait("scalar", ve, ve.v)
            E.dma("scalar", ys[oc * 128:(oc + 1) * 128, ts * TS:(ts + 1) * TS], y_st[bi][:, :], f"g_ys{bi}")
            E.buf_wait("scalar", ("ybf", bi))
            def _yst(s, bi=bi):
                s.activation(out=ybf_st[bi][:, :], in_=y_st[bi][:, :], func=AF.Copy)
                s.drain()
                s.activation(out=ysq_st[bi][:, :], in_=ybf_st[bi][:, :], func=AF.Square)
                return s.drain()
            E.op("scalar", _yst, inc=sc)
            E.buf_set(("y", bi), sc, f"g_ys{bi}")
            pend_stat.append((oc, bi, sc.v))
            if len(pend_stat) > 1:
                emit_stat()
        while pend_stat:
            emit_stat()
        E.wait("vector", pe)
        E.buf_wait("vector", "statrows")
        def _rows(v, ts=ts):
            sl = slice(ts * TS, (ts + 1) * TS)
            v.tensor_scalar(srow[0:1, :], psR[0:1, 0:TS], 1.0 / D, None, ALU.mult)
            v.tensor_scalar(sqrow[0:1, :], psAT[0:1, 0:TS], 1.0 / D, None, ALU.mult)
            v.drain()
            v.tensor_tensor(arow[0:1, sl], srow[0:1, :], srow[0:1, :], ALU.mult)
            v.drain()
            v.tensor_tensor(arow[0:1, sl], sqrow[0:1, :], arow[0:1, sl], ALU.subtract)
            return v.drain()
        E.op("vector", _rows, inc=ve)
        ps_set(psR, ve)
        ps_set(psAT, ve)
        E.wait("scalar", ve, ve.v)
        E.op("scalar", lambda s, ts=ts: s.activation(out=arow[0:1, ts * TS:(ts + 1) * TS],
                                                     in_=arow[0:1, ts * TS:(ts + 1) * TS],
                                                     func=AF.Sqrt, bias=consts[0:1, 0:1]))
        E.drain("scalar", inc=sc)
        E.wait("vector", sc, sc.v)
        def _rows2(v, ts=ts):
            sl = slice(ts * TS, (ts + 1) * TS)
            v.reciprocal(arow[0:1, sl], arow[0:1, sl])
            v.drain()
            v.tensor_tensor(brow[0:1, sl], srow[0:1, :], arow[0:1, sl], ALU.mult)
            return v.drain()
        E.op("vector", _rows2, inc=ve)
        E.buf_set("statrows", ve)

    for ts in range(NTS):
        ps_wait(psC)
        ps_wait(psD)
        E.wait("tensor", ve, ve.v)
        E.op("tensor", lambda t, ts=ts: t.matmul(psC[0:128, 0:TS], ones_r[:, :],
                                                 arow[0:1, ts * TS:(ts + 1) * TS], start=True, stop=True))
        E.op("tensor", lambda t, ts=ts: t.matmul(psD[0:128, 0:TS], ones_r[:, :],
                                                 brow[0:1, ts * TS:(ts + 1) * TS], start=True, stop=True),
             inc=pe)
        E.wait("vector", pe)
        E.buf_wait("vector", "bc12")
        def _cpbc(v):
            v.tensor_copy(bc1[:, :], psC[0:128, 0:TS])
            v.tensor_copy(bc2[:, :], psD[0:128, 0:TS])
            return v.drain()
        E.op("vector", _cpbc, inc=ve)
        ps_set(psC, ve)
        ps_set(psD, ve)
        for kc in range(KC):
            bi = kc % 2
            E.buf_wait("sync", ("y", bi))
            if ts == 0 and kc == 0:
                E.wait_s("sync", "g_ys0")
                E.wait_s("sync", "g_ys1")
            E.dma("sync", y_st[bi][:, :], ys[kc * 128:(kc + 1) * 128, ts * TS:(ts + 1) * TS], f"d_y{bi}")
            E.buf_wait("vector", ("xr", bi))
            E.buf_wait("vector", ("x3w", bi))
            E.wait_s("vector", f"d_y{bi}")
            def _n3(v, kc=kc, ts=ts, bi=bi):
                v.tensor_tensor(xr_st[bi][:, :], y_st[bi][:, :], bc1[:, :], ALU.mult)
                v.drain()
                v.tensor_tensor(x3w[bi][:, :], xr_st[bi][:, :], bc2[:, :], ALU.subtract)
                return v.drain()
            E.op("vector", _n3, inc=ve)
            E.buf_set(("y", bi), ve)
            E.buf_set(("xr", bi), ve)
            E.wait("scalar", ve, ve.v)
            E.dma("scalar", x3s[kc * 128:(kc + 1) * 128, ts * TS:(ts + 1) * TS], x3w[bi][:, :], f"g_x3w{bi}")
            E.buf_set(("x3w", bi), f"g_x3w{bi}")
        E.buf_set("bc12", ve)

    if debug is not None and debug[0] == "wexp":
        E.wait("sync", ccs2, 1)
        E.barrier(esem)
        rbase = debug[3] if len(debug) > 3 else 0
        for rr in range(16):
            E.dma("gpsimd", dbg[rr * 128:(rr + 1) * 128, :],
                  we_out[rbase + rr * 128:rbase + (rr + 1) * 128, :], "g_dbg")
    if debug is not None and debug[0] == "x3":
        E.wait_s("gpsimd", "g_x3w0")
        E.wait_s("gpsimd", "g_x3w1")
        for kc in range(KC):
            E.dma("gpsimd", dbg[kc * 128:(kc + 1) * 128, :], x3s[kc * 128:(kc + 1) * 128, :], "g_dbg")
    if debug is not None and debug[0] == "y":
        E.wait_s("gpsimd", "g_ys0")
        E.wait_s("gpsimd", "g_ys1")
        for kc in range(KC):
            E.dma("gpsimd", dbg[kc * 128:(kc + 1) * 128, :], ys[kc * 128:(kc + 1) * 128, :], "g_dbg")

    E.barrier(esem, exclude=("gpsimd",))
    l3.close()
    lmid.close()
    lps.close()

    l5 = ExitStack()
    STW = 1024
    NST = T // STW
    h_sb = l5.enter_context(nc.sbuf_tensor("h_sb", [128, IC * STW], BF16))
    x3t = l5.enter_context(nc.sbuf_tensor("x3t", [128, KC * STW], BF16))
    wg_st = [l5.enter_context(nc.sbuf_tensor(f"wg_{i}", [128, KC * 128], BF16)) for i in range(2)]
    wu_st = [l5.enter_context(nc.sbuf_tensor(f"wu_{i}", [128, KC * 128], BF16)) for i in range(2)]
    wd_st = [l5.enter_context(nc.sbuf_tensor(f"wd_{i}", [128, IC * 128], BF16)) for i in range(2)]
    sg_st = [l5.enter_context(nc.sbuf_tensor(f"sg_{i}", [128, STW], BF16)) for i in range(2)]
    yf_st = [l5.enter_context(nc.sbuf_tensor(f"yf_{i}", [128, STW], F32)) for i in range(2)]
    of_st = [l5.enter_context(nc.sbuf_tensor(f"of_{i}", [128, STW], F32)) for i in range(2)]
    ps6 = [l5.enter_context(nc.psum_tensor(f"ps6_{i}", [128, TS], F32)) for i in range(4)]
    psum8 = [psA, psB, psC, psD] + ps6

    E.wait("sync", ccs2, 1)
    for st in range(NST):
        E.buf_wait("sync", "x3t")
        for kc in range(KC):
            E.dma("sync", x3t[:, kc * STW:(kc + 1) * STW],
                  x3s[kc * 128:(kc + 1) * 128, st * STW:(st + 1) * STW], "d_x3t")
        for ic in range(IC):
            bi = ic % 2
            quad = psum8[0:4] if bi == 0 else psum8[4:8]
            E.buf_wait("sync", ("wgu", bi))
            for kc in range(KC):
                seg, krow = kc // 2, kc % 2
                goff = seg * SEGSZ + (ic * 2 + krow) * 16384
                uoff = seg * SEGSZ + _EG + (ic * 2 + krow) * 16384
                E.dma("sync", wg_st[bi][:, kc * 128:(kc + 1) * 128],
                      we_out[goff // 2048:goff // 2048 + 8, :], f"d_wg{bi}")
                E.dma("sync", wu_st[bi][:, kc * 128:(kc + 1) * 128],
                      we_out[uoff // 2048:uoff // 2048 + 8, :], f"d_wg{bi}")
            for p in quad:
                ps_wait(p)
            E.wait_s("tensor", f"d_wg{bi}")
            E.wait_s("tensor", "d_x3t")
            for kc in range(KC):
                for gu in range(2):
                    wsb = wg_st if gu == 0 else wu_st
                    for hf in range(2):
                        col = kc * STW + hf * TS
                        E.op("tensor", lambda t, kc=kc, hf=hf, bi=bi, col=col,
                             wsb=wsb, p=quad[gu * 2 + hf]: t.matmul(
                            p[:, :], wsb[bi][:, kc * 128:(kc + 1) * 128],
                            x3t[:, col:col + TS], start=(kc == 0), stop=(kc == KC - 1)),
                            inc=(pe if (kc == KC - 1 and gu == 1 and hf == 1) else None))
            E.buf_set(("wgu", bi), pe)
            if ic == IC - 1:
                E.buf_set("x3t", pe)
            E.wait("scalar", pe)
            E.buf_wait("scalar", ("sg", bi))
            def _sil(s, bi=bi, quad=quad):
                s.activation(out=sg_st[bi][:, 0:TS], in_=quad[0][:, :], func=AF.Silu)
                s.activation(out=sg_st[bi][:, TS:2 * TS], in_=quad[1][:, :], func=AF.Silu)
                return s.drain()
            E.op("scalar", _sil, inc=sc)
            ps_set(quad[0], sc)
            ps_set(quad[1], sc)
            E.wait("vector", sc, sc.v)
            E.buf_wait("vector", "h_sb")
            def _hm(v, ic=ic, bi=bi, quad=quad):
                v.tensor_tensor(h_sb[:, ic * STW:ic * STW + TS], sg_st[bi][:, 0:TS],
                                quad[2][:, :], ALU.mult)
                v.drain()
                v.tensor_tensor(h_sb[:, ic * STW + TS:(ic + 1) * STW], sg_st[bi][:, TS:2 * TS],
                                quad[3][:, :], ALU.mult)
                return v.drain()
            E.op("vector", _hm, inc=ve)
            ps_set(quad[2], ve)
            ps_set(quad[3], ve)
            E.buf_set(("sg", bi), ve)
        h_ready = ve.v
        if debug is not None and debug[0] == "h" and st == 0:
            E.wait("gpsimd", ve, ve.v)
            E.dma("gpsimd", dbg[:, :], h_sb[:, 0:debug[1][1]], "g_dbg")
        for oc in range(KC):
            bi = oc % 2
            pair = psum8[(oc % 4) * 2:(oc % 4) * 2 + 2]
            E.buf_wait("sync", ("wd", bi))
            for j in range(IC):
                seg, loc = _down_owner(j)
                doff = seg * SEGSZ + _EG + _EU + (oc * 6 + loc) * 16384
                E.dma("sync", wd_st[bi][:, j * 128:(j + 1) * 128],
                      we_out[doff // 2048:doff // 2048 + 8, :], f"d_wd{bi}")
            E.buf_wait("sync", ("yf", bi))
            E.dma("sync", yf_st[bi][:, :], ys[oc * 128:(oc + 1) * 128, st * STW:(st + 1) * STW], f"d_yf{bi}")
            ps_wait(pair[0])
            ps_wait(pair[1])
            E.wait_s("tensor", f"d_wd{bi}")
            E.wait("tensor", ve, h_ready)
            for j in range(IC):
                for hf in range(2):
                    E.op("tensor", lambda t, j=j, hf=hf, bi=bi, p=pair[hf]: t.matmul(
                        p[:, :], wd_st[bi][:, j * 128:(j + 1) * 128],
                        h_sb[:, j * STW + hf * TS:j * STW + (hf + 1) * TS],
                        start=(j == 0), stop=(j == IC - 1)),
                        inc=(pe if (j == IC - 1 and hf == 1) else None))
            E.buf_set(("wd", bi), pe)
            if oc == KC - 1:
                E.buf_set("h_sb", pe)
            E.wait("vector", pe)
            E.wait_s("vector", f"d_yf{bi}")
            E.buf_wait("vector", ("of", bi))
            def _dn(v, bi=bi, pair=pair):
                v.tensor_tensor(of_st[bi][:, 0:TS], pair[0][:, :], yf_st[bi][:, 0:TS], ALU.add)
                v.drain()
                v.tensor_tensor(of_st[bi][:, TS:2 * TS], pair[1][:, :], yf_st[bi][:, TS:2 * TS], ALU.add)
                return v.drain()
            E.op("vector", _dn, inc=ve)
            ps_set(pair[0], ve)
            ps_set(pair[1], ve)
            E.buf_set(("yf", bi), ve)
            E.wait("scalar", ve, ve.v)
            E.dma("scalar", out_ext[oc * 128:(oc + 1) * 128, st * STW:(st + 1) * STW],
                  of_st[bi][:, :], f"g_out{bi}")
            E.buf_set(("of", bi), f"g_out{bi}")

    if debug is not None:
        E.wait_s("gpsimd", "g_dbg")
    E.barrier(esem)

    with nc.Block() as block:
        @block.sync
        def _(e):
            for fn in E.q["sync"]:
                fn(e)

        @block.scalar
        def _(e):
            for fn in E.q["scalar"]:
                fn(e)

        @block.vector
        def _(e):
            for fn in E.q["vector"]:
                fn(e)

        @block.tensor
        def _(e):
            for fn in E.q["tensor"]:
                fn(e)

        @block.gpsimd
        def _(e):
            for fn in E.q["gpsimd"]:
                fn(e)

    l5.close()
    est.close()
    return nc


def _prep_inputs(inputs):
    x = np.asarray(inputs["L0_attn_hidden_states"])[0]
    slope = np.asarray(inputs["L0_attn_slope_rate"])
    w_qkv = np.asarray(inputs["L0_attn_w_qkv"])
    w_gate = np.asarray(inputs["L0_attn_w_gate"])
    w_out = np.asarray(inputs["L0_attn_w_out"])
    norm_w = np.asarray(inputs["L0_attn_norm_w"])
    lut = np.asarray(inputs["L0_ffn_lut"])
    gidx = np.asarray(inputs["L0_ffn_gate_idx"])
    uidx = np.asarray(inputs["L0_ffn_up_idx"])
    didx = np.asarray(inputs["L0_ffn_down_idx"])

    def chunks(wT, noc):
        return np.ascontiguousarray(
            wT.reshape(KC, 128, noc, 128).transpose(0, 2, 1, 3)).reshape(KC * noc * 128, 128)

    wqkv_c = chunks(np.ascontiguousarray(w_qkv.T), OC_QKV)
    wgate_c = chunks(np.ascontiguousarray(w_gate.T), KC)
    wout_c = chunks(np.ascontiguousarray(w_out.T), KC)
    normw_t = np.ascontiguousarray(norm_w.reshape(KC, 128).T)
    gidxT = np.ascontiguousarray(gidx.T)
    uidxT = np.ascontiguousarray(uidx.T)
    didxT = np.ascontiguousarray(didx.T)

    in_maps = []
    for c in range(NC):
        rows = np.zeros((1, 48), np.float32)
        rows[0, 0:16] = slope
        rows[0, 16:32] = lut
        for cp in range(NC):
            rows[0, 32 + cp] = float(c - 1 - cp) if cp < c else 0.0
            rows[0, 40 + cp] = 1.0 if cp < c else 0.0
        dsl = np.zeros((768, D), np.int32)
        r0 = _DSTART[c] * 128
        nreal = _DCHUNKS[c] * 128
        dsl[:nreal] = didxT[r0:r0 + nreal]
        m = {
            "x": np.ascontiguousarray(x[c * T:(c + 1) * T].T),
            "wqkv": wqkv_c,
            "wgate": wgate_c,
            "wout": wout_c,
            "normw": normw_t,
            "rows": rows,
            "gidx": np.ascontiguousarray(gidxT[c * 256:(c + 1) * 256]),
            "uidx": np.ascontiguousarray(uidxT[c * 256:(c + 1) * 256]),
            "didx": dsl,
        }
        in_maps.append(m)
    return in_maps


def _kernel_impl(inputs, debug=None):
    in_maps = _prep_inputs(inputs)
    nc = build_program(debug=debug)
    res = run_bass_kernel_spmd(nc, in_maps, core_ids=list(range(NC)))
    outs = [r["out"] for r in res.results]
    full = np.empty((1, SEQ, D), np.float32)
    for c in range(NC):
        full[0, c * T:(c + 1) * T, :] = outs[c].T
    if debug is not None:
        return full, [r["dbg"] for r in res.results]
    return full


def kernel(**inputs):
    return _kernel_impl(inputs)
